# version 16 (final)
# speedup vs baseline: 1.0544x; 1.0022x over previous
import numpy as np
import ml_dtypes
import concourse.bass as bass
import concourse.mybir as mybir
from concourse.bass_utils import run_bass_kernel_spmd


F32 = mybir.dt.float32
BF16 = mybir.dt.bfloat16
AF = mybir.ActivationFunctionType
ALU = mybir.AluOpType
AX = mybir.AxisListType

ENGINES = ("tensor", "vector", "scalar", "gpsimd", "sync")
DMA_QUEUES = ("sync", "gpsimd", "scalar")


class TV:
    __slots__ = ("h", "p0", "p1", "f0", "f1", "_ap")

    def __init__(self, h, p0=None, p1=None, f0=None, f1=None, ap=None):
        self.h = h
        shape = getattr(h, "shape", None)
        self.p0 = 0 if p0 is None else p0
        self.p1 = (shape[0] if shape is not None else 1 << 30) if p1 is None else p1
        self.f0 = 0 if f0 is None else f0
        self.f1 = (int(np.prod(shape[1:])) if shape is not None else 1 << 30) if f1 is None else f1
        self._ap = ap

    @property
    def name(self):
        return self.h if isinstance(self.h, str) else self.h.name

    @property
    def ap(self):
        if self._ap is not None:
            return self._ap
        return self.h.ap()[self.p0:self.p1, self.f0:self.f1]

    def overlaps(self, o):
        return (self.p0 < o.p1 and o.p0 < self.p1 and
                self.f0 < o.f1 and o.f0 < self.f1)

    def covers(self, o):
        return (self.p0 <= o.p0 and self.p1 >= o.p1 and
                self.f0 <= o.f0 and self.f1 >= o.f1)


def tv(h, p=None, f=None, ap=None):
    p0, p1 = (p if p is not None else (None, None))
    f0, f1 = (f if f is not None else (None, None))
    return TV(h, p0, p1, f0, f1, ap=ap)


class Prog:
    def __init__(self, nc):
        self.nc = nc
        self.ops = {e: [] for e in ENGINES}
        self.sems = {e: nc.alloc_semaphore("S_" + e) for e in ENGINES}
        self.cnt = {e: 0 for e in ENGINES}
        self.K = 6
        self.dsems = {}
        self.dcnt = {}
        self.dissue = {q: 0 for q in DMA_QUEUES}
        for q in DMA_QUEUES:
            for s in range(self.K):
                self.dsems[f"{q}{s}"] = nc.alloc_semaphore(f"D_{q}{s}")
                self.dcnt[f"{q}{s}"] = 0
        self.known = {e: {} for e in ENGINES}
        self.writes = {}
        self.reads = {}
        self.n_ops = 0

    def _need_tokens(self, eng, reads, writes):
        toks = {}
        for r in reads:
            for rect, tok in self.writes.get(r.name, ()):
                if rect.overlaps(r):
                    toks[tok[0]] = max(toks.get(tok[0], 0), tok[1])
        for w in writes:
            for rect, tok in self.writes.get(w.name, ()):
                if rect.overlaps(w):
                    toks[tok[0]] = max(toks.get(tok[0], 0), tok[1])
            for rect, tok in self.reads.get(w.name, ()):
                if rect.overlaps(w):
                    toks[tok[0]] = max(toks.get(tok[0], 0), tok[1])
        return toks

    def _filter_known(self, eng, toks):
        out = []
        kn = self.known[eng]
        for semkey, val in toks.items():
            if kn.get(semkey, 0) < val:
                out.append((semkey, val))
                kn[semkey] = val
        return out

    def _record(self, reads, writes, token):
        for r in reads:
            lst = self.reads.setdefault(r.name, [])
            lst[:] = [e for e in lst if not (r.covers(e[0]) and e[1][0] == token[0])]
            lst.append((r, token))
        for w in writes:
            wl = self.writes.setdefault(w.name, [])
            wl[:] = [e for e in wl if not w.covers(e[0])]
            wl.append((w, token))
            rl = self.reads.setdefault(w.name, [])
            rl[:] = [e for e in rl if not w.covers(e[0])]

    def op(self, eng, emit, reads=(), writes=(), extra_inc=None):
        toks = self._need_tokens(eng, reads, writes)
        waits = self._filter_known(eng, toks)
        self.cnt[eng] += 1
        token = ("S_" + eng, self.cnt[eng])
        self.ops[eng].append((waits, emit, [(token[0], 1)] + (extra_inc or [])))
        self._record(reads, writes, token)
        self.n_ops += 1
        return token

    def dma(self, queue, emit, reads=(), writes=()):
        toks = self._need_tokens(queue, reads, writes)
        slot = f"{queue}{self.dissue[queue] % self.K}"
        self.dissue[queue] += 1
        if self.dcnt[slot] > 0:
            toks["D_" + slot] = max(toks.get("D_" + slot, 0), self.dcnt[slot])
        waits = self._filter_known(queue, toks)
        self.dcnt[slot] += 16
        token = ("D_" + slot, self.dcnt[slot])
        self.ops[queue].append((waits, emit, [(token[0], 16)]))
        self._record(reads, writes, token)
        self.n_ops += 1
        return token

    def full_barrier(self):
        for e in ENGINES:
            self.wait_all(e)

    def sync_all_to(self, token):
        semkey, val = token
        for e in ENGINES:
            waits = self._filter_known(e, {semkey: val})
            if waits:
                self.ops[e].append((waits, None, []))

    def barrier_token(self, eng):
        return ("S_" + eng, self.cnt[eng])

    def wait_all(self, eng):
        toks = {}
        for e in ENGINES:
            if e != eng and self.cnt[e] > 0:
                toks["S_" + e] = self.cnt[e]
        for slot, c in self.dcnt.items():
            if c > 0:
                toks["D_" + slot] = c
        waits = self._filter_known(eng, toks)
        if waits:
            self.ops[eng].append((waits, None, []))

    def _semobj(self, key):
        kind, name = key.split("_", 1)
        return self.sems[name] if kind == "S" else self.dsems[name]

    def build(self):
        nc = self.nc
        with nc.Block() as blk:
            for eng_name in ENGINES:
                ops = self.ops[eng_name]
                if not ops:
                    continue

                def make_body(ops):
                    def body(eng):
                        for waits, emit, incs in ops:
                            for semkey, val in waits:
                                eng.wait_ge(self._semobj(semkey), val)
                            if emit is None:
                                continue
                            inst = emit(eng)
                            for semkey, val in incs:
                                inst.then_inc(self._semobj(semkey), val)
                    return body

                getattr(blk, eng_name)(make_body(ops))
        return blk


SBUF_BASE = 16512
SBUF_TOP = 229344


class Sbuf:
    def __init__(self, nc, base=SBUF_BASE):
        self.nc = nc
        self.off = base
        self.marks = {}

    def alloc(self, name, shape, dtype, align=64):
        esz = {F32: 4, BF16: 2}[dtype]
        nbytes = int(np.prod(shape[1:])) * esz
        off = (self.off + align - 1) // align * align
        h = self.nc.alloc_sbuf_tensor_at(name, list(shape), dtype, offset=off)
        self.off = off + nbytes
        assert self.off <= SBUF_TOP, f"SBUF overflow at {name}: {self.off}"
        return h

    def mark(self, key):
        self.marks[key] = self.off

    def reset(self, key):
        self.off = self.marks[key]


def stage_consts(p, nc, bufs, dram, fconst_vals):
    tok = p.dma('sync', lambda s: s.dma_start(out=bufs['fconsts'].ap(), in_=dram['fconsts'].ap()),
                reads=[tv('dram_const')], writes=[tv(bufs['fconsts'])])
    for i, v in enumerate(fconst_vals):
        nc.const_aps.register(F32, float(v), bufs['fconsts'].ap()[:, i:i+1])
    p.sync_all_to(tok)
    for nm in ['ident_f32', 'ident_bf16', 'ones_f32', 'ones_col_f32', 'ones_col_bf16', 'ones_bf16', 'esel', 'cmask', 'cm01T']:
        if nm in bufs:
            p.dma('sync', lambda s, nm=nm: s.dma_start(out=bufs[nm].ap(), in_=dram[nm].ap()),
                  reads=[tv('dram_const')], writes=[tv(bufs[nm])])


def row_to_col(p, bufs, dram_vec, idx, n, dst_tv, stage_rows=True):
    b1r = bufs['b1raw']; idf = bufs['ident_f32']; tps = bufs['tps']
    def emit(s):
        ap = dram_vec.ap()[idx] if idx is not None else dram_vec.ap()
        return s.dma_start(out=b1r.ap()[0:n, 0:128], in_=ap.rearrange("(r q) -> r q", q=128))
    p.dma('sync', emit, reads=[tv('dram_vec_' + dram_vec.name)], writes=[tv(b1r, p=(0, n))])
    p.op('tensor', lambda t: t.transpose(tps.ap()[0:128, 0:n], b1r.ap()[0:n, 0:128], idf.ap()[0:n, 0:n]),
         reads=[tv(b1r, p=(0, n)), tv(idf)], writes=[tv(bufs['tps'])])
    p.op('vector', lambda v: v.tensor_copy(out=dst_tv.ap, in_=tps.ap()[:, 0:n]),
         reads=[tv(bufs['tps'])], writes=[dst_tv])


def transpose_in(p, bufs, x_dram, T, CT, row0=0):
    X = bufs['X']; idf = bufs['ident_f32']
    tb2 = [bufs['tps'], bufs['gps']]
    xtok = bufs['xtok']
    NT = T // 128
    for tt in range(NT):
        xb = xtok[tt % 2]
        p.dma('sync', lambda s, tt=tt, xb=xb: s.dma_start(
                out=xb.ap(), in_=x_dram.ap()[row0+tt*128:row0+(tt+1)*128, :]),
              reads=[tv('dram_x')], writes=[tv(xb)])
        for ct in range(CT):
            tps = tb2[ct % 2]
            p.op('tensor', lambda t, ct=ct, xb=xb, tps=tps: t.transpose(
                    tps.ap()[:, 0:128], xb.ap()[:, ct*128:(ct+1)*128], idf.ap()),
                 reads=[tv(xb, f=(ct*128, (ct+1)*128)), tv(idf)], writes=[tv(tps)])
            p.op('vector', lambda v, ct=ct, tt=tt, tps=tps: v.tensor_copy(
                    out=X.ap()[:, ct*T+tt*128:ct*T+(tt+1)*128], in_=tps.ap()[:, 0:128]),
                 reads=[tv(tps)], writes=[tv(X, f=(ct*T+tt*128, ct*T+(tt+1)*128))])


def transpose_out(p, bufs, y_dram, T, CT, row0=0):
    X = bufs['X']; idf = bufs['ident_f32']
    tb2 = [bufs['tps'], bufs['gps']]
    xtok = bufs['xtok']
    NT = T // 128
    for tt in range(NT):
        xb = xtok[tt % 2]
        for ct in range(CT):
            tps = tb2[ct % 2]
            p.op('tensor', lambda t, ct=ct, tt=tt, tps=tps: t.transpose(
                    tps.ap()[:, 0:128], X.ap()[:, ct*T+tt*128:ct*T+(tt+1)*128], idf.ap()),
                 reads=[tv(X, f=(ct*T+tt*128, ct*T+(tt+1)*128)), tv(idf)], writes=[tv(tps)])
            p.op('vector', lambda v, ct=ct, xb=xb, tps=tps: v.tensor_copy(
                    out=xb.ap()[:, ct*128:(ct+1)*128], in_=tps.ap()[:, 0:128]),
                 reads=[tv(tps)], writes=[tv(xb, f=(ct*128, (ct+1)*128))])
        p.dma('sync', lambda s, tt=tt, xb=xb: s.dma_start(
                out=y_dram.ap()[row0+tt*128:row0+(tt+1)*128, :], in_=xb.ap()),
              reads=[tv(xb)], writes=[tv('dram_y')])



def l0ffn_ln_gates(p, bufs, dram, CT, T, NR, eps_ln=1e-5):
    eps_rms = float(np.finfo(np.float32).eps)
    X = bufs['X']
    u_hat = bufs['u_hat']
    st = bufs['lnstats']
    bca = bufs['lnbca']
    tmp2 = bufs['lntmp']
    xsq = bufs['lnxsq']
    onesc = bufs['ones_col_f32']; onesf = bufs['ones_f32']
    sps = bufs['sps']; gps = bufs['gps']
    g_sh = bufs['g_shared_sb']; cent = bufs['cent_sb']
    C = CT * 128
    inv_c = 1.0 / C
    Srow = lambda k: tv(st, p=(0, 1), f=(k*512, (k+1)*512))
    sap = lambda k: st.ap()[0:1, k*512:(k+1)*512]

    for tc in range(T // 512):
        f0 = tc * 512
        def emit(t, f0=f0):
            last = None
            for ct in range(CT):
                last = t.matmul(sps.ap()[0:1, 0:512], onesc.ap(),
                                X.ap()[:, ct*T+f0:ct*T+f0+512], start=(ct == 0), stop=(ct == CT-1))
            return last
        p.op('tensor', emit, reads=[tv(X), tv(onesc)], writes=[tv(sps)])
        p.op('vector', lambda v: v.tensor_copy(out=sap(0), in_=sps.ap()[0:1, 0:512]),
             reads=[tv(sps)], writes=[Srow(0)])
        for ct in range(CT):
            p.op('scalar', lambda s, ct=ct, f0=f0: s.activation(
                    out=xsq.ap(), in_=X.ap()[:, ct*T+f0:ct*T+f0+512], func=AF.Square),
                 reads=[tv(X, f=(ct*T+f0, ct*T+f0+512))], writes=[tv(xsq)])
            p.op('tensor', lambda t, ct=ct: t.matmul(
                    sps.ap()[0:1, 0:512], onesc.ap(), xsq.ap(),
                    start=(ct == 0), stop=(ct == CT-1)),
                 reads=[tv(xsq), tv(onesc)], writes=[tv(sps)])
        p.op('vector', lambda v: v.tensor_copy(out=sap(1), in_=sps.ap()[0:1, 0:512]),
             reads=[tv(sps)], writes=[Srow(1)])
        p.op('vector', lambda v: v.tensor_scalar(out=sap(0), in0=sap(0), scalar1=inv_c,
                                                 scalar2=None, op0=ALU.mult),
             reads=[Srow(0)], writes=[Srow(0)])
        p.op('vector', lambda v: v.tensor_mul(out=sap(2), in0=sap(0), in1=sap(0)),
             reads=[Srow(0)], writes=[Srow(2)])
        p.op('vector', lambda v: v.tensor_scalar(out=sap(1), in0=sap(1), scalar1=inv_c,
                                                 scalar2=None, op0=ALU.mult),
             reads=[Srow(1)], writes=[Srow(1)])
        p.op('vector', lambda v: v.tensor_sub(out=sap(1), in0=sap(1), in1=sap(2)),
             reads=[Srow(1), Srow(2)], writes=[Srow(1)])
        p.op('scalar', lambda s: s.activation(out=sap(2), in_=sap(1), func=AF.Ln, bias=eps_ln, scale=1.0),
             reads=[Srow(1)], writes=[Srow(2)])
        p.op('scalar', lambda s: s.activation(out=sap(3), in_=sap(2), func=AF.Exp, bias=0.0, scale=-0.5),
             reads=[Srow(2)], writes=[Srow(3)])
        p.op('vector', lambda v: v.tensor_mul(out=sap(2), in0=sap(1), in1=sap(3)),
             reads=[Srow(1), Srow(3)], writes=[Srow(2)])
        p.op('vector', lambda v: v.tensor_mul(out=sap(2), in0=sap(2), in1=sap(3)),
             reads=[Srow(2), Srow(3)], writes=[Srow(2)])
        p.op('scalar', lambda s: s.activation(out=sap(2), in_=sap(2), func=AF.Ln, bias=eps_rms, scale=1.0),
             reads=[Srow(2)], writes=[Srow(2)])
        p.op('scalar', lambda s: s.activation(out=sap(1), in_=sap(2), func=AF.Exp, bias=0.0, scale=-0.5),
             reads=[Srow(2)], writes=[Srow(1)])
        for (k, dst0) in [(0, 0), (3, 512), (1, 1024)]:
            p.op('tensor', lambda t, k=k: t.matmul(
                    sps.ap()[:, 0:512], onesf.ap()[0:1, 0:128], sap(k), start=True, stop=True),
                 reads=[Srow(k), tv(onesf)], writes=[tv(sps)])
            p.op('vector', lambda v, dst0=dst0: v.tensor_copy(
                    out=bca.ap()[:, dst0:dst0+512], in_=sps.ap()[:, 0:512]),
                 reads=[tv(sps)], writes=[tv(bca, f=(dst0, dst0+512))])
        NT = 4
        for ct in range(CT):
            h0 = (ct % 2) * 512
            p.op('vector', lambda v, ct=ct, h0=h0, f0=f0: v.tensor_sub(
                    out=tmp2.ap()[:, h0:h0+512], in0=X.ap()[:, ct*T+f0:ct*T+f0+512], in1=bca.ap()[:, 0:512]),
                 reads=[tv(X, f=(ct*T+f0, ct*T+f0+512)), tv(bca, f=(0, 512))],
                 writes=[tv(tmp2, f=(h0, h0+512))])
            p.op('vector', lambda v, h0=h0: v.tensor_mul(
                    out=tmp2.ap()[:, h0:h0+512], in0=tmp2.ap()[:, h0:h0+512], in1=bca.ap()[:, 512:1024]),
                 reads=[tv(tmp2, f=(h0, h0+512)), tv(bca, f=(512, 1024))],
                 writes=[tv(tmp2, f=(h0, h0+512))])
            p.op('vector', lambda v, ct=ct, h0=h0, f0=f0: v.scalar_tensor_tensor(
                    out=X.ap()[:, ct*T+f0:ct*T+f0+512], in0=tmp2.ap()[:, h0:h0+512], scalar=0.5,
                    in1=X.ap()[:, ct*T+f0:ct*T+f0+512], op0=ALU.mult, op1=ALU.add),
                 reads=[tv(tmp2, f=(h0, h0+512)), tv(X, f=(ct*T+f0, ct*T+f0+512))],
                 writes=[tv(X, f=(ct*T+f0, ct*T+f0+512))])
            p.op('vector', lambda v, ct=ct, h0=h0, f0=f0: v.scalar_tensor_tensor(
                    out=u_hat.ap()[:, ct*T+f0:ct*T+f0+512], in0=tmp2.ap()[:, h0:h0+512],
                    scalar=g_sh.ap()[:, ct:ct+1], in1=bca.ap()[:, 1024:1536],
                    op0=ALU.mult, op1=ALU.mult),
                 reads=[tv(tmp2, f=(h0, h0+512)), tv(g_sh, f=(ct, ct+1)), tv(bca, f=(1024, 1536))],
                 writes=[tv(u_hat, f=(ct*T+f0, ct*T+f0+512))])
            def emit(t, ct=ct, h0=h0, tc=tc):
                last = None
                for j in range(NT):
                    tt = tc * NT + j
                    last = t.matmul(gps.ap()[:, tt*8:tt*8+NR],
                                    tmp2.ap()[:, h0+j*128:h0+(j+1)*128],
                                    cent.ap()[:, ct*NR:(ct+1)*NR],
                                    start=(tc == 0 and ct == 0 and j == 0),
                                    stop=(tc == T//512-1 and ct == CT-1 and j == NT-1))
                return last
            p.op('tensor', emit, reads=[tv(tmp2, f=(h0, h0+512)), tv(cent)], writes=[tv(gps)])


def l0ffn_gates(p, bufs, T, NR):
    gps = bufs['gps']
    sc_tok = bufs['sc_tok']
    gw = bufs['gate_work']
    gatesT = bufs['gatesT']
    idf = bufs['ident_f32']
    tps = bufs['tps']
    NT = T // 128
    for tt in range(NT):
        c8 = (tt * 8, tt * 8 + NR)
        p.op('scalar', lambda s, tt=tt: s.activation(
                out=sc_tok.ap()[:, tt*8:tt*8+NR], in_=gps.ap()[:, tt*8:tt*8+NR],
                func=AF.Sigmoid, bias=0.0, scale=1.0),
             reads=[tv(gps)], writes=[tv(sc_tok, f=c8)])
        sv = tv(sc_tok, f=c8)
        p.op('vector', lambda v, tt=tt: v.reduce_max(out=gw.ap()[:, 0:1], in_=sc_tok.ap()[:, tt*8:tt*8+NR], axis=AX.X),
             reads=[sv], writes=[tv(gw, f=(0, 1))])
        p.op('vector', lambda v, tt=tt: v.tensor_scalar(out=gw.ap()[:, 8:8+NR], in0=sc_tok.ap()[:, tt*8:tt*8+NR],
                                                        scalar1=gw.ap()[:, 0:1], scalar2=-1e30,
                                                        op0=ALU.is_ge, op1=ALU.mult),
             reads=[sv, tv(gw, f=(0, 1))], writes=[tv(gw, f=(8, 8+NR))])
        p.op('vector', lambda v, tt=tt: v.tensor_add(out=gw.ap()[:, 16:16+NR], in0=sc_tok.ap()[:, tt*8:tt*8+NR],
                                                     in1=gw.ap()[:, 8:8+NR]),
             reads=[sv, tv(gw, f=(8, 8+NR))], writes=[tv(gw, f=(16, 16+NR))])
        p.op('vector', lambda v: v.reduce_max(out=gw.ap()[:, 1:2], in_=gw.ap()[:, 16:16+NR], axis=AX.X),
             reads=[tv(gw, f=(16, 16+NR))], writes=[tv(gw, f=(1, 2))])
        p.op('vector', lambda v, tt=tt: v.tensor_scalar(out=gw.ap()[:, 24:24+NR], in0=sc_tok.ap()[:, tt*8:tt*8+NR],
                                                        scalar1=gw.ap()[:, 1:2], scalar2=None, op0=ALU.is_ge),
             reads=[sv, tv(gw, f=(1, 2))], writes=[tv(gw, f=(24, 24+NR))])
        p.op('vector', lambda v, tt=tt: v.tensor_mul(out=gw.ap()[:, 32:32+NR], in0=sc_tok.ap()[:, tt*8:tt*8+NR],
                                                     in1=gw.ap()[:, 24:24+NR]),
             reads=[sv, tv(gw, f=(24, 24+NR))], writes=[tv(gw, f=(32, 32+NR))])
        p.op('vector', lambda v, tt=tt: v.reduce_sum(out=gw.ap()[:, 2:3], in_=sc_tok.ap()[:, tt*8:tt*8+NR], axis=AX.X),
             reads=[sv], writes=[tv(gw, f=(2, 3))])
        p.op('vector', lambda v: v.reciprocal(out=gw.ap()[:, 3:4], in_=gw.ap()[:, 2:3]),
             reads=[tv(gw, f=(2, 3))], writes=[tv(gw, f=(3, 4))])
        p.op('vector', lambda v: v.tensor_scalar(out=gw.ap()[:, 32:32+NR], in0=gw.ap()[:, 32:32+NR],
                                                 scalar1=gw.ap()[:, 3:4], scalar2=0.5,
                                                 op0=ALU.mult, op1=ALU.mult),
             reads=[tv(gw, f=(3, 4)), tv(gw, f=(32, 32+NR))], writes=[tv(gw, f=(32, 32+NR))])
        p.op('tensor', lambda t, tt=tt: t.transpose(tps.ap()[0:NR, 0:128], gw.ap()[:, 32:32+NR], idf.ap()[:, 0:128]),
             reads=[tv(gw, f=(32, 32+NR)), tv(idf)], writes=[tv(tps)])
        p.op('vector', lambda v, tt=tt: v.tensor_copy(out=gatesT.ap()[0:NR, tt*128:(tt+1)*128], in_=tps.ap()[0:NR, 0:128]),
             reads=[tv(tps)], writes=[tv(gatesT, p=(0, NR), f=(tt*128, (tt+1)*128))])


def l0ffn_experts(p, bufs, dram, CT, T, H, NS, NR, w1names, w2names, b1names, b2names,
                  cachenames, first):
    X, u_hat = bufs['X'], bufs['u_hat']
    W1b = bufs['W1buf']
    W2t = bufs['W2tile']
    h_buf = bufs['h_buf']
    b1b = bufs['b1buf']; b1r = bufs['b1raw']; b2col = bufs['b2col']
    evt2 = [bufs['evtmp'], bufs['evtmp2']]
    g_b = bufs['g_b']; gatesT = bufs['gatesT']
    onesb = bufs['ones_bf16']; idf = bufs['ident_f32']; esel = bufs['esel']
    ps = bufs['ps']
    hps = [ps[0], ps[1], ps[2], ps[3]]
    tps = bufs['tps']
    HT = H // 128
    HG = H // 256
    NTB = T // 512
    C = CT * 128
    w1cache, w2cache = cachenames

    experts = [('s', i) for i in range(NS)] + [('r', i) for i in range(NR)]
    for ei, (kind, e) in enumerate(experts):
        w1 = dram[w1names[0] if kind == 's' else w1names[1]]
        w2 = dram[w2names[0] if kind == 's' else w2names[1]]
        b1 = dram[b1names[0] if kind == 's' else b1names[1]]
        b2 = dram[b2names[0] if kind == 's' else b2names[1]]
        w1c = dram[w1cache[0] if kind == 's' else w1cache[1]]
        w2c = dram[w2cache[0] if kind == 's' else w2cache[1]]
        w1c_n = 'dram_' + (w1cache[0] if kind == 's' else w1cache[1])
        w2c_n = 'dram_' + (w2cache[0] if kind == 's' else w2cache[1])

        if kind == 'r' and e == 0:
            grat = bufs['gratio_sb']
            for ct in range(CT):
                p.op('vector', lambda v, ct=ct: v.tensor_scalar(
                        out=u_hat.ap()[:, ct*T:(ct+1)*T], in0=u_hat.ap()[:, ct*T:(ct+1)*T],
                        scalar1=grat.ap()[:, ct:ct+1], scalar2=None, op0=ALU.mult),
                     reads=[tv(u_hat, f=(ct*T, (ct+1)*T)), tv(grat, f=(ct, ct+1))],
                     writes=[tv(u_hat, f=(ct*T, (ct+1)*T))])

        p.dma('gpsimd', lambda s, b1=b1, e=e: s.dma_start(
                out=b1r.ap()[0:HT, 0:128], in_=b1.ap()[e].rearrange("(r q) -> r q", q=128)),
              reads=[tv('dram_b1')], writes=[tv(b1r, p=(0, HT))])
        p.op('tensor', lambda t: t.transpose(
                tps.ap()[0:128, 0:HT], b1r.ap()[0:HT, 0:128], idf.ap()[0:HT, 0:HT]),
             reads=[tv(b1r, p=(0, HT)), tv(idf)], writes=[tv(tps)])
        p.op('vector', lambda v: v.tensor_copy(out=b1b.ap()[:, 0:HT], in_=tps.ap()[:, 0:HT]),
             reads=[tv(tps)], writes=[tv(b1b)])
        row_to_col(p, bufs, b2, e, CT, tv(b2col))
        if kind == 'r':
            for tc in range(T // 512):
                p.op('tensor', lambda t, e=e, tc=tc: t.matmul(
                        tps.ap()[:, 0:512], esel.ap()[0:NR, e*128:(e+1)*128],
                        gatesT.ap()[0:NR, tc*512:(tc+1)*512], start=True, stop=True),
                     reads=[tv(gatesT, p=(0, NR), f=(tc*512, (tc+1)*512)), tv(esel)], writes=[tv(tps)])
                p.op('vector', lambda v, tc=tc: v.tensor_copy(out=g_b.ap()[:, tc*512:(tc+1)*512], in_=tps.ap()[:, 0:512]),
                     reads=[tv(tps)], writes=[tv(g_b, f=(tc*512, (tc+1)*512))])

        for tb in range(NTB):
            t0 = tb * 512
            cast_now = first and tb == 0
            for hg in range(HG):
                wbuf = W1b[(tb * HG + hg) % len(W1b)]
                if cast_now:
                    p.dma('gpsimd', lambda g, w1=w1, e=e, hg=hg, wbuf=wbuf: g.dma_start(
                            out=wbuf.ap().rearrange("q (ct h) -> q ct h", h=256),
                            in_=w1.ap()[e].rearrange("(ct q) h -> q ct h", q=128)[:, :, hg*256:(hg+1)*256]),
                          reads=[tv('dram_w1')], writes=[tv(wbuf)])
                    p.dma('scalar', lambda s, w1c=w1c, e=e, hg=hg, wbuf=wbuf: s.dma_start(
                            out=w1c.ap()[e][:, hg*CT*256:(hg+1)*CT*256], in_=wbuf.ap()),
                          reads=[tv(wbuf)], writes=[tv(w1c_n)])
                else:
                    p.dma('sync', lambda s, w1c=w1c, e=e, hg=hg, wbuf=wbuf: s.dma_start(
                            out=wbuf.ap(), in_=w1c.ap()[e][:, hg*CT*256:(hg+1)*CT*256]),
                          reads=[tv(w1c_n)], writes=[tv(wbuf)])
                pp2 = [hps[(hg * 2 + i) % 4] for i in range(2)]
                def emit(t, wbuf=wbuf, t0=t0, pp2=pp2):
                    last = None
                    for i in range(2):
                        for ct in range(CT):
                            last = t.matmul(pp2[i].ap(), wbuf.ap()[:, ct*256+i*128:ct*256+(i+1)*128],
                                            u_hat.ap()[:, ct*T+t0:ct*T+t0+512],
                                            start=(ct == 0), stop=(ct == CT-1))
                    return last
                p.op('tensor', emit, reads=[tv(wbuf), tv(u_hat)],
                     writes=[tv(pp2[0]), tv(pp2[1])])
                for i in range(2):
                    ht = hg * 2 + i
                    pp = pp2[i]
                    p.op('scalar', lambda s, ht=ht, pp=pp: s.activation(
                            out=h_buf.ap()[:, ht*512:(ht+1)*512], in_=pp.ap(),
                            func=AF.Gelu, bias=b1b.ap()[:, ht:ht+1], scale=1.0),
                         reads=[tv(pp), tv(b1b, f=(ht, ht+1))], writes=[tv(h_buf, f=(ht*512, (ht+1)*512))])
            for hp in range(HT // 2):
                ht0 = hp * 2
                w2b = W2t[hp % len(W2t)]
                if cast_now:
                    p.dma('gpsimd', lambda g, w2=w2, e=e, ht0=ht0, w2b=w2b: g.dma_start(
                            out=w2b.ap().rearrange("q (i c) -> q i c", i=2),
                            in_=w2.ap()[e].rearrange("(i q) c -> q i c", q=128)[:, ht0:ht0+2, :]),
                          reads=[tv('dram_w2')], writes=[tv(w2b)])
                    p.dma('scalar', lambda s, w2c=w2c, e=e, ht0=ht0, w2b=w2b: s.dma_start(
                            out=w2c.ap()[e][:, ht0*C:(ht0+2)*C], in_=w2b.ap()),
                          reads=[tv(w2b)], writes=[tv(w2c_n)])
                else:
                    p.dma('sync', lambda s, w2c=w2c, e=e, ht0=ht0, w2b=w2b: s.dma_start(
                            out=w2b.ap(), in_=w2c.ap()[e][:, ht0*C:(ht0+2)*C]),
                          reads=[tv(w2c_n)], writes=[tv(w2b)])
                def emit(t, ht0=ht0, w2b=w2b):
                    last = None
                    for i in range(2):
                        ht = ht0 + i
                        for ct in range(CT):
                            last = t.matmul(ps[ct].ap(), w2b.ap()[:, i*C+ct*128:i*C+(ct+1)*128],
                                            h_buf.ap()[:, ht*512:(ht+1)*512],
                                            start=(ht == 0), stop=(ht == HT-1))
                    return last
                p.op('tensor', emit,
                     reads=[tv(w2b), tv(h_buf, f=(ht0*512, (ht0+2)*512))],
                     writes=[tv(ps[ct2]) for ct2 in range(CT)])
            for ct in range(CT):
                pp = ps[ct]
                tmp = evt2[ct % 2]
                if kind == 's':
                    p.op('vector', lambda v, ct=ct, pp=pp, tmp=tmp: v.tensor_scalar(
                            out=tmp.ap(), in0=pp.ap(), scalar1=b2col.ap()[:, ct:ct+1],
                            scalar2=0.5, op0=ALU.add, op1=ALU.mult),
                         reads=[tv(pp), tv(b2col, f=(ct, ct+1))], writes=[tv(tmp)])
                    p.op('vector', lambda v, ct=ct, t0=t0, tmp=tmp: v.tensor_add(
                            out=X.ap()[:, ct*T+t0:ct*T+t0+512], in0=X.ap()[:, ct*T+t0:ct*T+t0+512], in1=tmp.ap()),
                         reads=[tv(tmp), tv(X, f=(ct*T+t0, ct*T+t0+512))],
                         writes=[tv(X, f=(ct*T+t0, ct*T+t0+512))])
                else:
                    p.op('vector', lambda v, ct=ct, pp=pp, tmp=tmp: v.tensor_scalar(
                            out=tmp.ap(), in0=pp.ap(), scalar1=b2col.ap()[:, ct:ct+1],
                            scalar2=None, op0=ALU.add),
                         reads=[tv(pp), tv(b2col, f=(ct, ct+1))], writes=[tv(tmp)])
                    p.op('vector', lambda v, t0=t0, tmp=tmp: v.tensor_mul(
                            out=tmp.ap(), in0=tmp.ap(), in1=g_b.ap()[:, t0:t0+512]),
                         reads=[tv(tmp), tv(g_b, f=(t0, t0+512))], writes=[tv(tmp)])
                    p.op('vector', lambda v, ct=ct, t0=t0, tmp=tmp: v.tensor_add(
                            out=X.ap()[:, ct*T+t0:ct*T+t0+512], in0=X.ap()[:, ct*T+t0:ct*T+t0+512], in1=tmp.ap()),
                         reads=[tv(tmp), tv(X, f=(ct*T+t0, ct*T+t0+512))],
                         writes=[tv(X, f=(ct*T+t0, ct*T+t0+512))])


def l0ffn_phase(p, bufs, dram, CT, T, H, NS, NR, first=True):
    l0ffn_ln_gates(p, bufs, dram, CT, T, NR)
    p.full_barrier()
    l0ffn_gates(p, bufs, T, NR)
    p.full_barrier()
    l0ffn_experts(p, bufs, dram, CT, T, H, NS, NR,
                  ('W1_s', 'W1_r'), ('W2_s', 'W2_r'), ('b1_s', 'b1_r'), ('b2_s', 'b2_r'),
                  (('w1bf_s', 'w1bf_r'), ('w2bf_s', 'w2bf_r')), first)


USE_DMA_TRANSPOSE = False


def spill_X(p, bufs, dram, CT, T):
    X = bufs['X']
    p.dma('sync', lambda s: s.dma_start(out=dram['xspill'].ap(), in_=X.ap()),
          reads=[tv(X)], writes=[tv('dram_xspill')])


def unspill_add(p, bufs, dram, CT, T, half=True):
    X = bufs['X']
    xre = bufs['xre']
    for ct in range(CT):
        for tc in range(T // 512):
            xb = xre[(ct * (T // 512) + tc) % 2]
            f0 = ct * T + tc * 512
            p.dma('sync', lambda s, f0=f0, xb=xb: s.dma_start(
                    out=xb.ap(), in_=dram['xspill'].ap()[:, f0:f0+512]),
                  reads=[tv('dram_xspill')], writes=[tv(xb)])
            sc = 0.5 if half else 1.0
            p.op('vector', lambda v, f0=f0, xb=xb, sc=sc: v.scalar_tensor_tensor(
                    out=X.ap()[:, f0:f0+512], in0=X.ap()[:, f0:f0+512], scalar=sc,
                    in1=xb.ap(), op0=ALU.mult, op1=ALU.add),
                 reads=[tv(X, f=(f0, f0+512)), tv(xb)], writes=[tv(X, f=(f0, f0+512))])


def rms_phase(p, bufs, CT, T, gain_key, dst_key, eps=1e-6):
    X = bufs['X']
    dst = bufs[dst_key]
    st = bufs['lnstats']; bca = bufs['lnbca']; xsq = bufs['lnxsq']
    onesc = bufs['ones_col_f32']; onesf = bufs['ones_f32']
    sps = bufs['sps']
    gain = bufs[gain_key]
    C = CT * 128
    inv_c = 1.0 / C
    Srow = lambda k: tv(st, p=(0, 1), f=(k*512, (k+1)*512))
    sap = lambda k: st.ap()[0:1, k*512:(k+1)*512]
    for tc in range(T // 512):
        f0 = tc * 512
        for ct in range(CT):
            p.op('scalar', lambda s, ct=ct, f0=f0: s.activation(
                    out=xsq.ap(), in_=X.ap()[:, ct*T+f0:ct*T+f0+512], func=AF.Square),
                 reads=[tv(X, f=(ct*T+f0, ct*T+f0+512))], writes=[tv(xsq)])
            p.op('tensor', lambda t, ct=ct: t.matmul(
                    sps.ap()[0:1, 0:512], onesc.ap(), xsq.ap(),
                    start=(ct == 0), stop=(ct == CT-1)),
                 reads=[tv(xsq), tv(onesc)], writes=[tv(sps)])
        p.op('vector', lambda v: v.tensor_copy(out=sap(1), in_=sps.ap()[0:1, 0:512]),
             reads=[tv(sps)], writes=[Srow(1)])
        p.op('vector', lambda v: v.tensor_scalar(out=sap(1), in0=sap(1), scalar1=inv_c,
                                                 scalar2=None, op0=ALU.mult),
             reads=[Srow(1)], writes=[Srow(1)])
        p.op('scalar', lambda s, eps=eps: s.activation(out=sap(2), in_=sap(1), func=AF.Ln, bias=eps, scale=1.0),
             reads=[Srow(1)], writes=[Srow(2)])
        p.op('scalar', lambda s: s.activation(out=sap(3), in_=sap(2), func=AF.Exp, bias=0.0, scale=-0.5),
             reads=[Srow(2)], writes=[Srow(3)])
        p.op('tensor', lambda t: t.matmul(
                sps.ap()[:, 0:512], onesf.ap()[0:1, 0:128], sap(3), start=True, stop=True),
             reads=[Srow(3), tv(onesf)], writes=[tv(sps)])
        p.op('vector', lambda v: v.tensor_copy(out=bca.ap()[:, 0:512], in_=sps.ap()[:, 0:512]),
             reads=[tv(sps)], writes=[tv(bca, f=(0, 512))])
        for ct in range(CT):
            p.op('vector', lambda v, ct=ct, f0=f0: v.scalar_tensor_tensor(
                    out=dst.ap()[:, ct*T+f0:ct*T+f0+512], in0=X.ap()[:, ct*T+f0:ct*T+f0+512],
                    scalar=gain.ap()[:, ct:ct+1], in1=bca.ap()[:, 0:512],
                    op0=ALU.mult, op1=ALU.mult),
                 reads=[tv(X, f=(ct*T+f0, ct*T+f0+512)), tv(gain, f=(ct, ct+1)), tv(bca, f=(0, 512))],
                 writes=[tv(dst, f=(ct*T+f0, ct*T+f0+512))])


def ln_plain_inplace(p, bufs, CT, T, eps=1e-5):
    X = bufs['X']
    st = bufs['lnstats']; bca = bufs['lnbca']; tmp2 = bufs['lntmp']; xsq = bufs['lnxsq']
    onesc = bufs['ones_col_f32']; onesf = bufs['ones_f32']
    sps = bufs['sps']
    C = CT * 128
    inv_c = 1.0 / C
    Srow = lambda k: tv(st, p=(0, 1), f=(k*512, (k+1)*512))
    sap = lambda k: st.ap()[0:1, k*512:(k+1)*512]
    for tc in range(T // 512):
        f0 = tc * 512
        def emit(t, f0=f0):
            last = None
            for ct in range(CT):
                last = t.matmul(sps.ap()[0:1, 0:512], onesc.ap(),
                                X.ap()[:, ct*T+f0:ct*T+f0+512], start=(ct == 0), stop=(ct == CT-1))
            return last
        p.op('tensor', emit, reads=[tv(X), tv(onesc)], writes=[tv(sps)])
        p.op('vector', lambda v: v.tensor_copy(out=sap(0), in_=sps.ap()[0:1, 0:512]),
             reads=[tv(sps)], writes=[Srow(0)])
        for ct in range(CT):
            p.op('scalar', lambda s, ct=ct, f0=f0: s.activation(
                    out=xsq.ap(), in_=X.ap()[:, ct*T+f0:ct*T+f0+512], func=AF.Square),
                 reads=[tv(X, f=(ct*T+f0, ct*T+f0+512))], writes=[tv(xsq)])
            p.op('tensor', lambda t, ct=ct: t.matmul(
                    sps.ap()[0:1, 0:512], onesc.ap(), xsq.ap(),
                    start=(ct == 0), stop=(ct == CT-1)),
                 reads=[tv(xsq), tv(onesc)], writes=[tv(sps)])
        p.op('vector', lambda v: v.tensor_copy(out=sap(1), in_=sps.ap()[0:1, 0:512]),
             reads=[tv(sps)], writes=[Srow(1)])
        p.op('vector', lambda v: v.tensor_scalar(out=sap(0), in0=sap(0), scalar1=inv_c,
                                                 scalar2=None, op0=ALU.mult),
             reads=[Srow(0)], writes=[Srow(0)])
        p.op('vector', lambda v: v.tensor_mul(out=sap(2), in0=sap(0), in1=sap(0)),
             reads=[Srow(0)], writes=[Srow(2)])
        p.op('vector', lambda v: v.tensor_scalar(out=sap(1), in0=sap(1), scalar1=inv_c,
                                                 scalar2=None, op0=ALU.mult),
             reads=[Srow(1)], writes=[Srow(1)])
        p.op('vector', lambda v: v.tensor_sub(out=sap(1), in0=sap(1), in1=sap(2)),
             reads=[Srow(1), Srow(2)], writes=[Srow(1)])
        p.op('scalar', lambda s, eps=eps: s.activation(out=sap(2), in_=sap(1), func=AF.Ln, bias=eps, scale=1.0),
             reads=[Srow(1)], writes=[Srow(2)])
        p.op('scalar', lambda s: s.activation(out=sap(3), in_=sap(2), func=AF.Exp, bias=0.0, scale=-0.5),
             reads=[Srow(2)], writes=[Srow(3)])
        for (k, dst0) in [(0, 0), (3, 512)]:
            p.op('tensor', lambda t, k=k: t.matmul(
                    sps.ap()[:, 0:512], onesf.ap()[0:1, 0:128], sap(k), start=True, stop=True),
                 reads=[Srow(k), tv(onesf)], writes=[tv(sps)])
            p.op('vector', lambda v, dst0=dst0: v.tensor_copy(
                    out=bca.ap()[:, dst0:dst0+512], in_=sps.ap()[:, 0:512]),
                 reads=[tv(sps)], writes=[tv(bca, f=(dst0, dst0+512))])
        for ct in range(CT):
            h0 = (ct % 2) * 512
            p.op('vector', lambda v, ct=ct, h0=h0, f0=f0: v.tensor_sub(
                    out=tmp2.ap()[:, h0:h0+512], in0=X.ap()[:, ct*T+f0:ct*T+f0+512], in1=bca.ap()[:, 0:512]),
                 reads=[tv(X, f=(ct*T+f0, ct*T+f0+512)), tv(bca, f=(0, 512))],
                 writes=[tv(tmp2, f=(h0, h0+512))])
            p.op('vector', lambda v, ct=ct, h0=h0, f0=f0: v.tensor_mul(
                    out=X.ap()[:, ct*T+f0:ct*T+f0+512], in0=tmp2.ap()[:, h0:h0+512], in1=bca.ap()[:, 512:1024]),
                 reads=[tv(tmp2, f=(h0, h0+512)), tv(bca, f=(512, 1024))],
                 writes=[tv(X, f=(ct*T+f0, ct*T+f0+512))])


def l1ffn_attn(p, bufs, dram, CT, T, first):
    X = bufs['X']; xn = bufs['xn']
    qT, kT = bufs['qT'], bufs['kT']
    v_tok = bufs['v_tok']
    out_cT = bufs['out_cT']
    wq1 = bufs['wq1']
    ow_b = bufs['ow_b']
    idf = bufs['ident_f32']
    denr = bufs['gate_work']
    ps = bufs['ps']; sps = bufs['sps']
    qkvT = dram['qkv_wT']
    owT = dram['o_wT']
    qkvc = dram['qkv_bf']
    owc = dram['o_bf']
    NT = T // 128
    NCH = T // 512

    for g in range(4):
        def load_w(j, base, g=g):
            wb = wq1
            c0 = (base + g*256) * 8
            c1 = (base + (g+1)*256) * 8
            if first:
                p.dma('gpsimd', lambda s, base=base, g=g, wb=wb: s.dma_start(
                        out=wb.ap().rearrange("q (ct o) -> q ct o", o=256),
                        in_=qkvT.ap().rearrange("(ct q) o -> q ct o", q=128)[:, :, base+g*256:base+(g+1)*256]),
                      reads=[tv('dram_qkvT')], writes=[tv(wb)])
                p.dma('scalar', lambda s, c0=c0, c1=c1, wb=wb: s.dma_start(
                        out=qkvc.ap()[:, c0:c1], in_=wb.ap()),
                      reads=[tv(wb)], writes=[tv('dram_qkv_bf')])
            else:
                p.dma('sync', lambda s, c0=c0, c1=c1, wb=wb: s.dma_start(
                        out=wb.ap(), in_=qkvc.ap()[:, c0:c1]),
                      reads=[tv('dram_qkv_bf')], writes=[tv(wb)])
            return wb
        for j, base, dst in [(0, 0, qT), (1, 1024, kT)]:
            wb = load_w(j, base)
            scale = 0.125 if j == 0 else 1.0
            for r in range(2):
                for tc in range(NCH):
                    pb = [sps, ps[6]][(r * NCH + tc) % 2]
                    def emit(t, wb=wb, r=r, tc=tc, pb=pb):
                        last = None
                        for ct in range(CT):
                            last = t.matmul(pb.ap(), wb.ap()[:, ct*256+r*128:ct*256+(r+1)*128],
                                            xn.ap()[:, ct*T+tc*512:ct*T+(tc+1)*512],
                                            start=(ct == 0), stop=(ct == CT-1))
                        return last
                    p.op('tensor', emit, reads=[tv(wb), tv(xn)], writes=[tv(pb)])
                    p.op('scalar', lambda s, dst=dst, r=r, tc=tc, scale=scale, pb=pb: s.activation(
                            out=dst.ap()[:, r*T+tc*512:r*T+(tc+1)*512], in_=pb.ap(),
                            func=AF.Copy, bias=0.0, scale=scale),
                         reads=[tv(pb)], writes=[tv(dst, f=(r*T+tc*512, r*T+(tc+1)*512))])
        wb = load_w(2, 2048)
        for tt in range(NT):
            pb = [ps[5], ps[7]][tt % 2]
            def emit(t, wb=wb, tt=tt, pb=pb):
                last = None
                for ct in range(CT):
                    last = t.matmul(pb.ap()[:, 0:256],
                                    xn.ap()[:, ct*T+tt*128:ct*T+(tt+1)*128],
                                    wb.ap()[:, ct*256:(ct+1)*256],
                                    start=(ct == 0), stop=(ct == CT-1))
                return last
            p.op('tensor', emit, reads=[tv(wb), tv(xn)], writes=[tv(pb)])
            p.op('vector', lambda v, tt=tt, pb=pb: v.tensor_copy(out=v_tok.ap()[:, tt*256:(tt+1)*256],
                                                                 in_=pb.ap()[:, 0:256]),
                 reads=[tv(pb)], writes=[tv(v_tok, f=(tt*256, (tt+1)*256))])
        eT = bufs['eT']
        ot = bufs['ot_sb']
        cm01 = bufs['cm01T']
        idb = bufs['ident_bf16']
        onescb = bufs['ones_col_bf16']
        SB = [ps[5], ps[6]]
        AB = [ps[7], ps[3]]
        TB = [ps[4], ps[2]]
        iters = [(hl, ti) for hl in range(4) for ti in range(NT)]
        units = []
        for it, (hl, ti) in enumerate(iters):
            nb = ti + 1
            ng = (nb + 3) // 4
            for g2 in range(ng):
                units.append((it, hl, ti, g2*4, min(g2*4+4, nb), g2 == ng-1))
        U = len(units)

        def sc_unit(u):
            it, hl, ti, j0, j1, lastg = units[u]
            r = hl // 2
            b = (hl % 2) * 64
            sbk = SB[u % 2]
            def emit(t, j0=j0, j1=j1, sbk=sbk, r=r, b=b, ti=ti):
                last = None
                for j2 in range(j0, j1):
                    last = t.matmul(sbk.ap()[:, (j2-j0)*128:(j2-j0+1)*128],
                                    kT.ap()[b:b+64, r*T+j2*128:r*T+(j2+1)*128],
                                    qT.ap()[b:b+64, r*T+ti*128:r*T+(ti+1)*128],
                                    start=(j2 == j0), stop=(j2 == j1-1))
                return last
            p.op('tensor', emit,
                 reads=[tv(kT, p=(b, b+64), f=(r*T+j0*128, r*T+j1*128)),
                        tv(qT, p=(b, b+64), f=(r*T+ti*128, r*T+(ti+1)*128))],
                 writes=[tv(sbk)])

        def rest_unit(u, gctr):
            it, hl, ti, j0, j1, lastg = units[u]
            nb = ti + 1
            sbk = SB[u % 2]
            ebf = eT[gctr % 4]
            ab = AB[it % 2]
            w = (j1 - j0) * 128
            p.op('scalar', lambda s, sbk=sbk, ebf=ebf, w=w: s.activation(
                    out=ebf.ap()[:, 0:w], in_=sbk.ap()[:, 0:w], func=AF.Exp, bias=0.0, scale=1.0),
                 reads=[tv(sbk)], writes=[tv(ebf, f=(0, w))])
            if lastg:
                off = (ti - j0) * 128
                p.op('vector', lambda v, ebf=ebf, off=off: v.tensor_mul(
                        out=ebf.ap()[:, off:off+128], in0=ebf.ap()[:, off:off+128], in1=cm01.ap()),
                     reads=[tv(ebf, f=(off, off+128)), tv(cm01)], writes=[tv(ebf, f=(off, off+128))])
            def emitav(t, j0=j0, j1=j1, ebf=ebf, ab=ab, hl=hl, nb=nb):
                last = None
                for j2 in range(j0, j1):
                    t.matmul(ab.ap()[:, 0:64],
                             ebf.ap()[:, (j2-j0)*128:(j2-j0+1)*128],
                             v_tok.ap()[:, j2*256+hl*64:j2*256+(hl+1)*64],
                             start=(j2 == 0), stop=False)
                    last = t.matmul(ab.ap()[:, 64:65],
                                    ebf.ap()[:, (j2-j0)*128:(j2-j0+1)*128],
                                    onescb.ap(),
                                    start=False, stop=(j2 == nb-1))
                return last
            p.op('tensor', emitav, reads=[tv(ebf, f=(0, w)), tv(v_tok), tv(onescb)], writes=[tv(ab)])

        def tail_unit(u):
            it, hl, ti, j0, j1, lastg = units[u]
            h = g * 4 + hl
            octile = h // 2
            ob = (h % 2) * 64
            ab = AB[it % 2]
            tb = TB[it % 2]
            osb = ot[it % 2]
            dc = 40 + (it % 2)
            p.op('vector', lambda v, ab=ab, dc=dc: v.reciprocal(
                    out=denr.ap()[:, dc:dc+1], in_=ab.ap()[:, 64:65]),
                 reads=[tv(ab)], writes=[tv(denr, f=(dc, dc+1))])
            p.op('vector', lambda v, ab=ab, osb=osb, dc=dc: v.tensor_scalar(
                    out=osb.ap()[:, 0:64], in0=ab.ap()[:, 0:64],
                    scalar1=denr.ap()[:, dc:dc+1], scalar2=None, op0=ALU.mult),
                 reads=[tv(ab), tv(denr, f=(dc, dc+1))], writes=[tv(osb, f=(0, 64))])
            p.op('tensor', lambda t, tb=tb, osb=osb: t.transpose(
                    tb.ap()[0:64, 0:128], osb.ap()[:, 0:64], idf.ap()),
                 reads=[tv(osb, f=(0, 64)), tv(idb)], writes=[tv(tb)])
            p.op('vector', lambda v, tb=tb, octile=octile, ob=ob, ti=ti: v.tensor_copy(
                    out=out_cT.ap()[ob:ob+64, octile*T+ti*128:octile*T+(ti+1)*128],
                    in_=tb.ap()[0:64, 0:128]),
                 reads=[tv(tb)],
                 writes=[tv(out_cT, p=(ob, ob+64), f=(octile*T+ti*128, octile*T+(ti+1)*128))])

        sc_unit(0)
        pend = None
        for u in range(U):
            if u + 1 < U:
                sc_unit(u + 1)
            rest_unit(u, u)
            if pend is not None:
                tail_unit(pend)
                pend = None
            if units[u][5]:
                pend = u
        if pend is not None:
            tail_unit(pend)
    for ot in range(CT):
        if first:
            p.dma('gpsimd', lambda s, ot=ot: s.dma_start(
                    out=ow_b.ap().rearrange("q (ct o) -> q ct o", o=128),
                    in_=owT.ap().rearrange("(ct q) o -> q ct o", q=128)[:, :, ot*128:(ot+1)*128]),
                  reads=[tv('dram_owT')], writes=[tv(ow_b)])
            p.dma('scalar', lambda s, ot=ot: s.dma_start(
                    out=owc.ap()[:, ot*1024:(ot+1)*1024], in_=ow_b.ap()),
                  reads=[tv(ow_b)], writes=[tv('dram_o_bf')])
        else:
            p.dma('sync', lambda s, ot=ot: s.dma_start(
                    out=ow_b.ap(), in_=owc.ap()[:, ot*1024:(ot+1)*1024]),
                  reads=[tv('dram_o_bf')], writes=[tv(ow_b)])
        for tc in range(NCH):
            def emit(t, ot=ot, tc=tc):
                last = None
                for ct in range(CT):
                    last = t.matmul(sps.ap(), ow_b.ap()[:, ct*128:(ct+1)*128],
                                    out_cT.ap()[:, ct*T+tc*512:ct*T+(tc+1)*512],
                                    start=(ct == 0), stop=(ct == CT-1))
                return last
            p.op('tensor', emit, reads=[tv(ow_b), tv(out_cT)], writes=[tv(sps)])
            p.op('vector', lambda v, ot=ot, tc=tc: v.tensor_add(
                    out=X.ap()[:, ot*T+tc*512:ot*T+(tc+1)*512],
                    in0=X.ap()[:, ot*T+tc*512:ot*T+(tc+1)*512], in1=sps.ap()),
                 reads=[tv(sps), tv(X, f=(ot*T+tc*512, ot*T+(tc+1)*512))],
                 writes=[tv(X, f=(ot*T+tc*512, ot*T+(tc+1)*512))])


def l1ffn_moe(p, bufs, dram, CT, T, H, first):
    X = bufs['X']; xn = bufs['xn']
    h_buf = bufs['h_buf']
    W1b = bufs['W1buf']; W2t = bufs['W2tile']
    F_b = bufs['g_b']
    gw = bufs['gate_work']; sc_tok = bufs['sc_tok']
    idf = bufs['ident_f32']
    gtw = bufs['gtw']
    fac = bufs['fac']; facrow = bufs['facrow']
    tmp = bufs['evtmp']
    ps = bufs['ps']; tps = bufs['tps']
    upT = dram['up_wT']; downT = dram['down_wT']
    upc = dram['up_bf']; downc = dram['down_bf']
    HT = H // 128
    HG = H // 256
    NTB = T // 512
    NT = T // 128
    C = CT * 128

    p.op('vector', lambda v: v.memset(fac.ap(), 0.0), reads=[], writes=[tv(fac)])
    for tt in range(NT):
        def emit(t, tt=tt):
            last = None
            for ct in range(CT):
                last = t.matmul(ps[3].ap()[:, tt*8:(tt+1)*8],
                                xn.ap()[:, ct*T+tt*128:ct*T+(tt+1)*128],
                                gtw.ap()[:, ct*8:(ct+1)*8],
                                start=(tt == 0 and ct == 0),
                                stop=(tt == NT-1 and ct == CT-1))
            return last
        p.op('tensor', emit, reads=[tv(xn), tv(gtw)], writes=[tv(ps[3])])
    for tt in range(NT):
        p.op('vector', lambda v, tt=tt: v.reduce_max(out=gw.ap()[:, 0:1], in_=ps[3].ap()[:, tt*8:(tt+1)*8],
                                                     axis=AX.X, negate=True),
             reads=[tv(ps[3])], writes=[tv(gw, f=(0, 1))])
        p.op('scalar', lambda s, tt=tt: s.activation(
                out=sc_tok.ap()[:, 0:8], in_=ps[3].ap()[:, tt*8:(tt+1)*8], func=AF.Exp,
                bias=gw.ap()[:, 0:1], scale=1.0, accum_out=gw.ap()[:, 1:2]),
             reads=[tv(ps[3]), tv(gw, f=(0, 1))], writes=[tv(sc_tok, f=(0, 8)), tv(gw, f=(1, 2))])
        p.op('vector', lambda v: v.reduce_max(out=gw.ap()[:, 2:3], in_=sc_tok.ap()[:, 0:8], axis=AX.X),
             reads=[tv(sc_tok, f=(0, 8))], writes=[tv(gw, f=(2, 3))])
        p.op('vector', lambda v: v.tensor_scalar(out=gw.ap()[:, 8:16], in0=sc_tok.ap()[:, 0:8],
                                                 scalar1=gw.ap()[:, 2:3], scalar2=-1e30,
                                                 op0=ALU.is_ge, op1=ALU.mult),
             reads=[tv(sc_tok, f=(0, 8)), tv(gw, f=(2, 3))], writes=[tv(gw, f=(8, 16))])
        p.op('vector', lambda v: v.tensor_add(out=gw.ap()[:, 16:24], in0=sc_tok.ap()[:, 0:8], in1=gw.ap()[:, 8:16]),
             reads=[tv(sc_tok, f=(0, 8)), tv(gw, f=(8, 16))], writes=[tv(gw, f=(16, 24))])
        p.op('vector', lambda v: v.reduce_max(out=gw.ap()[:, 3:4], in_=gw.ap()[:, 16:24], axis=AX.X),
             reads=[tv(gw, f=(16, 24))], writes=[tv(gw, f=(3, 4))])
        p.op('vector', lambda v: v.tensor_add(out=gw.ap()[:, 4:5], in0=gw.ap()[:, 2:3], in1=gw.ap()[:, 3:4]),
             reads=[tv(gw, f=(2, 4))], writes=[tv(gw, f=(4, 5))])
        p.op('vector', lambda v: v.reciprocal(out=gw.ap()[:, 5:6], in_=gw.ap()[:, 1:2]),
             reads=[tv(gw, f=(1, 2))], writes=[tv(gw, f=(5, 6))])
        p.op('vector', lambda v: v.tensor_mul(out=gw.ap()[:, 4:5], in0=gw.ap()[:, 4:5], in1=gw.ap()[:, 5:6]),
             reads=[tv(gw, f=(4, 6))], writes=[tv(gw, f=(4, 5))])
        p.op('vector', lambda v: v.tensor_scalar(out=gw.ap()[:, 6:7], in0=gw.ap()[:, 4:5],
                                                 scalar1=1e-9, scalar2=None, op0=ALU.add),
             reads=[tv(gw, f=(4, 5))], writes=[tv(gw, f=(6, 7))])
        p.op('vector', lambda v: v.reciprocal(out=gw.ap()[:, 7:8], in_=gw.ap()[:, 6:7]),
             reads=[tv(gw, f=(6, 7))], writes=[tv(gw, f=(7, 8))])
        p.op('vector', lambda v, tt=tt: v.tensor_mul(out=fac.ap()[:, tt:tt+1], in0=gw.ap()[:, 4:5],
                                                     in1=gw.ap()[:, 7:8]),
             reads=[tv(gw, f=(4, 8))], writes=[tv(fac, f=(tt, tt+1))])
    facrow = bufs['facrow']
    onesf = bufs['ones_f32']
    for tt in range(NT):
        p.op('tensor', lambda t, tt=tt: t.matmul(
                tps.ap()[0:1, 0:128], fac.ap()[:, tt:tt+1], idf.ap(),
                start=True, stop=True),
             reads=[tv(fac, f=(tt, tt+1)), tv(idf)], writes=[tv(tps)])
        p.op('vector', lambda v, tt=tt: v.tensor_copy(out=facrow.ap()[0:1, tt*128:(tt+1)*128],
                                                      in_=tps.ap()[0:1, 0:128]),
             reads=[tv(tps)], writes=[tv(facrow, p=(0, 1), f=(tt*128, (tt+1)*128))])
    for tc in range(T // 512):
        p.op('tensor', lambda t, tc=tc: t.matmul(
                tps.ap()[:, 0:512], onesf.ap()[0:1, 0:128],
                facrow.ap()[0:1, tc*512:(tc+1)*512], start=True, stop=True),
             reads=[tv(facrow, p=(0, 1), f=(tc*512, (tc+1)*512)), tv(onesf)], writes=[tv(tps)])
        p.op('vector', lambda v, tc=tc: v.tensor_copy(out=F_b.ap()[:, tc*512:(tc+1)*512],
                                                      in_=tps.ap()[:, 0:512]),
             reads=[tv(tps)], writes=[tv(F_b, f=(tc*512, (tc+1)*512))])

    for tb in range(NTB):
        t0 = tb * 512
        cast_now = first and tb == 0
        for hg in range(HG):
            wbuf = W1b[(tb * HG + hg) % 3]
            if cast_now:
                p.dma('gpsimd', lambda g, hg=hg, wbuf=wbuf: g.dma_start(
                        out=wbuf.ap().rearrange("q (ct h) -> q ct h", h=256),
                        in_=upT.ap().rearrange("(ct q) h -> q ct h", q=128)[:, :, hg*256:(hg+1)*256]),
                      reads=[tv('dram_upT')], writes=[tv(wbuf)])
                p.dma('scalar', lambda s, hg=hg, wbuf=wbuf: s.dma_start(
                        out=upc.ap()[:, hg*CT*256:(hg+1)*CT*256], in_=wbuf.ap()),
                      reads=[tv(wbuf)], writes=[tv('dram_up_bf')])
            else:
                p.dma('sync', lambda s, hg=hg, wbuf=wbuf: s.dma_start(
                        out=wbuf.ap(), in_=upc.ap()[:, hg*CT*256:(hg+1)*CT*256]),
                      reads=[tv('dram_up_bf')], writes=[tv(wbuf)])
            pp2 = [ps[(hg * 2 + i) % 4] for i in range(2)]
            def emit(t, wbuf=wbuf, t0=t0, pp2=pp2):
                last = None
                for i in range(2):
                    for ct in range(CT):
                        last = t.matmul(pp2[i].ap(), wbuf.ap()[:, ct*256+i*128:ct*256+(i+1)*128],
                                        xn.ap()[:, ct*T+t0:ct*T+t0+512],
                                        start=(ct == 0), stop=(ct == CT-1))
                return last
            p.op('tensor', emit, reads=[tv(wbuf), tv(xn)], writes=[tv(pp2[0]), tv(pp2[1])])
            for i in range(2):
                ht = hg * 2 + i
                pp = pp2[i]
                p.op('scalar', lambda s, ht=ht, pp=pp: s.activation(
                        out=h_buf.ap()[:, ht*512:(ht+1)*512], in_=pp.ap(), func=AF.Silu),
                     reads=[tv(pp)], writes=[tv(h_buf, f=(ht*512, (ht+1)*512))])
        for hp in range(HT // 2):
            ht0 = hp * 2
            w2b = W2t[hp % 3]
            if cast_now:
                p.dma('gpsimd', lambda g, ht0=ht0, w2b=w2b: g.dma_start(
                        out=w2b.ap().rearrange("q (i c) -> q i c", i=2),
                        in_=downT.ap().rearrange("(i q) c -> q i c", q=128)[:, ht0:ht0+2, :]),
                      reads=[tv('dram_downT')], writes=[tv(w2b)])
                p.dma('scalar', lambda s, ht0=ht0, w2b=w2b: s.dma_start(
                        out=downc.ap()[:, ht0*C:(ht0+2)*C], in_=w2b.ap()),
                      reads=[tv(w2b)], writes=[tv('dram_down_bf')])
            else:
                p.dma('sync', lambda s, ht0=ht0, w2b=w2b: s.dma_start(
                        out=w2b.ap(), in_=downc.ap()[:, ht0*C:(ht0+2)*C]),
                      reads=[tv('dram_down_bf')], writes=[tv(w2b)])
            def emit(t, ht0=ht0, w2b=w2b):
                last = None
                for i in range(2):
                    ht = ht0 + i
                    for ct in range(CT):
                        last = t.matmul(ps[ct].ap(), w2b.ap()[:, i*C+ct*128:i*C+(ct+1)*128],
                                        h_buf.ap()[:, ht*512:(ht+1)*512],
                                        start=(ht == 0), stop=(ht == HT-1))
                return last
            p.op('tensor', emit,
                 reads=[tv(w2b), tv(h_buf, f=(ht0*512, (ht0+2)*512))],
                 writes=[tv(ps[ct2]) for ct2 in range(CT)])
        for ct in range(CT):
            pp = ps[ct]
            p.op('vector', lambda v, t0=t0, pp=pp: v.tensor_mul(
                    out=tmp.ap(), in0=pp.ap(), in1=F_b.ap()[:, t0:t0+512]),
                 reads=[tv(pp), tv(F_b, f=(t0, t0+512))], writes=[tv(tmp)])
            p.op('vector', lambda v, ct=ct, t0=t0: v.tensor_add(
                    out=X.ap()[:, ct*T+t0:ct*T+t0+512], in0=X.ap()[:, ct*T+t0:ct*T+t0+512], in1=tmp.ap()),
                 reads=[tv(tmp), tv(X, f=(ct*T+t0, ct*T+t0+512))],
                 writes=[tv(X, f=(ct*T+t0, ct*T+t0+512))])


def l1ffn_phase(p, bufs, dram, CT, T, H, first=True):
    spill_X(p, bufs, dram, CT, T)
    ln_plain_inplace(p, bufs, CT, T)
    rms_phase(p, bufs, CT, T, 'n1w_sb', 'xn')
    p.full_barrier()
    l1ffn_attn(p, bufs, dram, CT, T, first)
    p.full_barrier()
    rms_phase(p, bufs, CT, T, 'n2w_sb', 'xn')
    p.full_barrier()
    p.dma('gpsimd', lambda g: g.dma_start(
            out=bufs['gtw'].ap().rearrange("q (ct e) -> q ct e", e=8),
            in_=dram['gate_wT'].ap().rearrange("(ct q) e -> q ct e", q=128)),
          reads=[tv('dram_gate')], writes=[tv(bufs['gtw'])])
    l1ffn_moe(p, bufs, dram, CT, T, H, first)
    p.full_barrier()
    unspill_add(p, bufs, dram, CT, T, half=True)



def l0attn_phase(p, bufs, dram, CT, T, first=True):
    X = bufs['X']
    xn = bufs['xn']
    kvb = bufs['kv_bf']
    qsT = bufs['qsT']
    ksT = bufs['ksT']
    v_tok = bufs['v_tok']
    out_cT = bufs['out_cT']
    qtok = bufs['qtok']
    qinv = bufs['qinv']
    qsc_b = bufs['qsc_b']
    denr = bufs['gate_work']
    idf = bufs['ident_f32']
    cmask = bufs['cmask']
    onesf = bufs['ones_f32']
    ps = bufs['ps']; sps = bufs['sps']; tps = bufs['tps']
    NT = T // 128
    NCH = T // 512
    C = CT * 128
    Wq = dram['L0_Wq']; Wkv = dram['L0_Wkv']; Wk = dram['L0_Wk']; Wv = dram['L0_Wv']; Wo = dram['L0_Wo']
    wqc = dram['l0_wq_bf']; woc = dram['l0_wo_bf']

    qg_r = bufs['qg_row']
    kg_r = bufs['kg_row']
    p.dma('sync', lambda s: s.dma_start(out=qg_r.ap(), in_=dram['L0_q_g'].ap().rearrange("(a d) -> a d", a=1)),
          reads=[tv('dram_qg')], writes=[tv(qg_r)])
    p.op('vector', lambda v: v.memset(kg_r.ap(), 0.0), reads=[], writes=[tv(kg_r)])
    p.dma('sync', lambda s: s.dma_start(out=kg_r.ap()[0:1, 0:32], in_=dram['L0_k_g'].ap().rearrange("(a d) -> a d", a=1)),
          reads=[tv('dram_kg')], writes=[tv(kg_r, f=(0, 32))])
    p.op('vector', lambda v: v.tensor_mul(out=kg_r.ap(), in0=kg_r.ap(), in1=qg_r.ap()),
         reads=[tv(kg_r), tv(qg_r)], writes=[tv(kg_r)])
    p.op('vector', lambda v: v.tensor_scalar(out=kg_r.ap(), in0=kg_r.ap(), scalar1=0.125,
                                             scalar2=None, op0=ALU.mult),
         reads=[tv(kg_r)], writes=[tv(kg_r)])
    for h in range(16):
        p.op('tensor', lambda t, h=h: t.matmul(
                tps.ap()[:, 0:64], onesf.ap()[0:1, 0:128], kg_r.ap(),
                start=True, stop=True),
             reads=[tv(kg_r), tv(onesf)], writes=[tv(tps)])
        p.op('vector', lambda v, h=h: v.tensor_copy(out=qsc_b.ap()[:, h*64:(h+1)*64],
                                                    in_=tps.ap()[:, 0:64]),
             reads=[tv(tps)], writes=[tv(qsc_b, f=(h*64, (h+1)*64))])

    wkvb = bufs['wkv_bf']
    p.dma('gpsimd', lambda g: g.dma_start(
            out=wkvb.ap().rearrange("q (ct o) -> q ct o", o=128),
            in_=Wkv.ap().rearrange("(ct q) o -> q ct o", q=128)),
          reads=[tv('dram_wkv')], writes=[tv(wkvb)])
    for tc in range(NCH):
        def emit(t, tc=tc):
            last = None
            for ct in range(CT):
                last = t.matmul(sps.ap(), wkvb.ap()[:, ct*128:(ct+1)*128],
                                xn.ap()[:, ct*T+tc*512:ct*T+(tc+1)*512],
                                start=(ct == 0), stop=(ct == CT-1))
            return last
        p.op('tensor', emit, reads=[tv(wkvb), tv(xn)], writes=[tv(sps)])
        p.op('scalar', lambda s, tc=tc: s.activation(out=kvb.ap()[:, tc*512:(tc+1)*512], in_=sps.ap(),
                                                     func=AF.Copy),
             reads=[tv(sps)], writes=[tv(kvb, f=(tc*512, (tc+1)*512))])

    wkb = bufs['wk_bf']; wvb = bufs['wv_bf']
    p.dma('gpsimd', lambda g: g.dma_start(out=wkb.ap(), in_=Wk.ap()),
          reads=[tv('dram_wk')], writes=[tv(wkb)])
    p.dma('gpsimd', lambda g: g.dma_start(out=wvb.ap(), in_=Wv.ap()),
          reads=[tv('dram_wv')], writes=[tv(wvb)])

    p.op('vector', lambda v: v.memset(ksT.ap(), 0.0), reads=[], writes=[tv(ksT)])

    wqh = bufs['wq0']
    qsq = bufs['qsq']
    for half in range(2):
        if first:
            p.dma('gpsimd', lambda g, half=half: g.dma_start(
                    out=wqh.ap().rearrange("q (ct o) -> q ct o", o=512),
                    in_=Wq.ap().rearrange("(ct q) o -> q ct o", q=128)[:, :, half*512:(half+1)*512]),
                  reads=[tv('dram_wq')], writes=[tv(wqh)])
            p.dma('scalar', lambda s, half=half: s.dma_start(
                    out=wqc.ap()[:, half*CT*512:(half+1)*CT*512], in_=wqh.ap()),
                  reads=[tv(wqh)], writes=[tv('dram_l0wq_bf')])
        else:
            p.dma('sync', lambda s, half=half: s.dma_start(
                    out=wqh.ap(), in_=wqc.ap()[:, half*CT*512:(half+1)*CT*512]),
                  reads=[tv('dram_l0wq_bf')], writes=[tv(wqh)])
        for tt in range(NT):
            def emit(t, tt=tt):
                last = None
                for ct in range(CT):
                    last = t.matmul(sps.ap(), xn.ap()[:, ct*T+tt*128:ct*T+(tt+1)*128],
                                    wqh.ap()[:, ct*512:(ct+1)*512],
                                    start=(ct == 0), stop=(ct == CT-1))
                return last
            p.op('tensor', emit, reads=[tv(wqh), tv(xn)], writes=[tv(sps)])
            p.op('vector', lambda v: v.tensor_copy(out=qtok.ap()[:, 0:512], in_=sps.ap()),
                 reads=[tv(sps)], writes=[tv(qtok, f=(0, 512))])
            p.op('scalar', lambda s: s.activation(out=qsq.ap()[:, 0:512], in_=qtok.ap()[:, 0:512], func=AF.Square),
                 reads=[tv(qtok, f=(0, 512))], writes=[tv(qsq, f=(0, 512))])
            for hl in range(8):
                p.op('vector', lambda v, hl=hl: v.reduce_sum(
                        out=qinv.ap()[:, hl:hl+1], in_=qsq.ap()[:, hl*64:(hl+1)*64], axis=AX.X),
                     reads=[tv(qsq, f=(hl*64, (hl+1)*64))], writes=[tv(qinv, f=(hl, hl+1))])
            p.op('vector', lambda v: v.tensor_scalar(out=qinv.ap()[:, 0:8], in0=qinv.ap()[:, 0:8],
                                                     scalar1=1.0/64, scalar2=None, op0=ALU.mult),
                 reads=[tv(qinv, f=(0, 8))], writes=[tv(qinv, f=(0, 8))])
            p.op('scalar', lambda s: s.activation(out=qinv.ap()[:, 16:24], in_=qinv.ap()[:, 0:8],
                                                  func=AF.Ln, bias=1e-6, scale=1.0),
                 reads=[tv(qinv, f=(0, 8))], writes=[tv(qinv, f=(16, 24))])
            p.op('scalar', lambda s: s.activation(out=qinv.ap()[:, 0:8], in_=qinv.ap()[:, 16:24],
                                                  func=AF.Exp, bias=0.0, scale=-0.5),
                 reads=[tv(qinv, f=(16, 24))], writes=[tv(qinv, f=(0, 8))])
            for hl in range(8):
                h = half * 8 + hl
                p.op('vector', lambda v, hl=hl, h=h: v.scalar_tensor_tensor(
                        out=qsq.ap()[:, hl*64:(hl+1)*64], in0=qtok.ap()[:, hl*64:(hl+1)*64],
                        scalar=qinv.ap()[:, hl:hl+1], in1=qsc_b.ap()[:, h*64:(h+1)*64],
                        op0=ALU.mult, op1=ALU.mult),
                     reads=[tv(qtok, f=(hl*64, (hl+1)*64)), tv(qinv, f=(hl, hl+1)),
                            tv(qsc_b, f=(h*64, (h+1)*64))],
                     writes=[tv(qsq, f=(hl*64, (hl+1)*64))])
            for cl in range(4):
                ct = half * 4 + cl
                tp2 = ps[6 + cl % 2]
                p.op('tensor', lambda t, cl=cl, tp2=tp2: t.transpose(
                        tp2.ap()[:, 0:128], qsq.ap()[:, cl*128:(cl+1)*128], idf.ap()),
                     reads=[tv(qsq, f=(cl*128, (cl+1)*128)), tv(idf)], writes=[tv(tp2)])
                p.op('vector', lambda v, ct=ct, tt=tt, tp2=tp2: v.tensor_copy(
                        out=qsT.ap()[:, ct*T+tt*128:ct*T+(tt+1)*128], in_=tp2.ap()[:, 0:128]),
                     reads=[tv(tp2)], writes=[tv(qsT, f=(ct*T+tt*128, ct*T+(tt+1)*128))])

    for tt in range(NT):
        def emitk(t, tt=tt):
            return t.matmul(sps.ap(), kvb.ap()[:, tt*128:(tt+1)*128], wkb.ap(),
                            start=True, stop=True)
        p.op('tensor', emitk, reads=[tv(kvb, f=(tt*128, (tt+1)*128)), tv(wkb)], writes=[tv(sps)])
        p.op('vector', lambda v: v.tensor_copy(out=qtok.ap()[:, 0:512], in_=sps.ap()),
             reads=[tv(sps)], writes=[tv(qtok, f=(0, 512))])
        p.op('scalar', lambda s: s.activation(out=qsq.ap()[:, 0:512], in_=qtok.ap()[:, 0:512], func=AF.Square),
             reads=[tv(qtok, f=(0, 512))], writes=[tv(qsq, f=(0, 512))])
        for h in range(16):
            p.op('vector', lambda v, h=h: v.reduce_sum(
                    out=qinv.ap()[:, h:h+1], in_=qsq.ap()[:, h*32:(h+1)*32], axis=AX.X),
                 reads=[tv(qsq, f=(h*32, (h+1)*32))], writes=[tv(qinv, f=(h, h+1))])
        p.op('vector', lambda v: v.tensor_scalar(out=qinv.ap()[:, 0:16], in0=qinv.ap()[:, 0:16],
                                                 scalar1=1.0/32, scalar2=None, op0=ALU.mult),
             reads=[tv(qinv, f=(0, 16))], writes=[tv(qinv, f=(0, 16))])
        p.op('scalar', lambda s: s.activation(out=qinv.ap()[:, 16:32], in_=qinv.ap()[:, 0:16],
                                              func=AF.Ln, bias=1e-6, scale=1.0),
             reads=[tv(qinv, f=(0, 16))], writes=[tv(qinv, f=(16, 32))])
        p.op('scalar', lambda s: s.activation(out=qinv.ap()[:, 0:16], in_=qinv.ap()[:, 16:32],
                                              func=AF.Exp, bias=0.0, scale=-0.5),
             reads=[tv(qinv, f=(16, 32))], writes=[tv(qinv, f=(0, 16))])
        qbf = qsq
        p.op('vector', lambda v: v.memset(qbf.ap(), 0.0), reads=[], writes=[tv(bufs['qsq'])])
        for h in range(16):
            p.op('vector', lambda v, h=h: v.tensor_scalar(
                    out=qbf.ap()[:, h*64:h*64+32], in0=qtok.ap()[:, h*32:(h+1)*32],
                    scalar1=qinv.ap()[:, h:h+1], scalar2=None, op0=ALU.mult),
                 reads=[tv(qtok, f=(h*32, (h+1)*32)), tv(qinv, f=(h, h+1))],
                 writes=[tv(bufs['qsq'], f=(h*64, h*64+32))])
        for ct in range(CT):
            tp2 = ps[6 + ct % 2]
            p.op('tensor', lambda t, ct=ct, tp2=tp2: t.transpose(
                    tp2.ap()[:, 0:128], qbf.ap()[:, ct*128:(ct+1)*128], idf.ap()),
                 reads=[tv(bufs['qsq'], f=(ct*128, (ct+1)*128)), tv(idf)], writes=[tv(tp2)])
            p.op('vector', lambda v, ct=ct, tt=tt, tp2=tp2: v.tensor_copy(
                    out=ksT.ap()[:, ct*T+tt*128:ct*T+(tt+1)*128], in_=tp2.ap()[:, 0:128]),
                 reads=[tv(tp2)], writes=[tv(ksT, f=(ct*T+tt*128, ct*T+(tt+1)*128))])
        def emitv(t, tt=tt):
            return t.matmul(ps[5].ap(), kvb.ap()[:, tt*128:(tt+1)*128], wvb.ap(),
                            start=True, stop=True)
        p.op('tensor', emitv, reads=[tv(kvb, f=(tt*128, (tt+1)*128)), tv(wvb)], writes=[tv(ps[5])])
        p.op('vector', lambda v, tt=tt: v.tensor_copy(out=v_tok.ap()[:, tt*512:(tt+1)*512], in_=ps[5].ap()),
             reads=[tv(ps[5])], writes=[tv(v_tok, f=(tt*512, (tt+1)*512))])

    p.full_barrier()
    p.op('vector', lambda v: v.memset(out_cT.ap(), 0.0), reads=[], writes=[tv(out_cT)])
    eT = bufs['eT']
    ot = bufs['ot_sb']
    cm01 = bufs['cm01T']
    idb = bufs['ident_bf16']
    onescb = bufs['ones_col_bf16']
    SB = [ps[5], ps[6]]
    AB = [ps[7], ps[3]]
    TB = [ps[4], ps[2]]
    iters = [(h, ti) for h in range(16) for ti in range(NT)]
    units = []
    for it, (h, ti) in enumerate(iters):
        nb = ti + 1
        ng = (nb + 3) // 4
        for g in range(ng):
            units.append((it, h, ti, g*4, min(g*4+4, nb), g == ng-1))
    U = len(units)

    def sc_unit(u):
        it, h, ti, j0, j1, lastg = units[u]
        pt = h // 2
        b = (h % 2) * 64
        sbk = SB[u % 2]
        def emit(t, j0=j0, j1=j1, sbk=sbk, pt=pt, b=b, ti=ti):
            last = None
            for j2 in range(j0, j1):
                last = t.matmul(sbk.ap()[:, (j2-j0)*128:(j2-j0+1)*128],
                                ksT.ap()[b:b+64, pt*T+j2*128:pt*T+(j2+1)*128],
                                qsT.ap()[b:b+64, pt*T+ti*128:pt*T+(ti+1)*128],
                                start=(j2 == j0), stop=(j2 == j1-1))
            return last
        p.op('tensor', emit,
             reads=[tv(ksT, p=(b, b+64), f=(pt*T+j0*128, pt*T+j1*128)),
                    tv(qsT, p=(b, b+64), f=(pt*T+ti*128, pt*T+(ti+1)*128))],
             writes=[tv(sbk)])

    def rest_unit(u):
        it, h, ti, j0, j1, lastg = units[u]
        nb = ti + 1
        sbk = SB[u % 2]
        ebf = eT[u % 4]
        ab = AB[it % 2]
        w = (j1 - j0) * 128
        p.op('scalar', lambda s, sbk=sbk, ebf=ebf, w=w: s.activation(
                out=ebf.ap()[:, 0:w], in_=sbk.ap()[:, 0:w], func=AF.Exp, bias=0.0, scale=1.0),
             reads=[tv(sbk)], writes=[tv(ebf, f=(0, w))])
        if lastg:
            off = (ti - j0) * 128
            p.op('vector', lambda v, ebf=ebf, off=off: v.tensor_mul(
                    out=ebf.ap()[:, off:off+128], in0=ebf.ap()[:, off:off+128], in1=cm01.ap()),
                 reads=[tv(ebf, f=(off, off+128)), tv(cm01)], writes=[tv(ebf, f=(off, off+128))])
        def emitav(t, j0=j0, j1=j1, ebf=ebf, ab=ab, h=h, nb=nb):
            last = None
            for j2 in range(j0, j1):
                t.matmul(ab.ap()[:, 0:32],
                         ebf.ap()[:, (j2-j0)*128:(j2-j0+1)*128],
                         v_tok.ap()[:, j2*512+h*32:j2*512+(h+1)*32],
                         start=(j2 == 0), stop=False)
                last = t.matmul(ab.ap()[:, 32:33],
                                ebf.ap()[:, (j2-j0)*128:(j2-j0+1)*128],
                                onescb.ap(),
                                start=False, stop=(j2 == nb-1))
            return last
        p.op('tensor', emitav, reads=[tv(ebf, f=(0, w)), tv(v_tok), tv(onescb)], writes=[tv(ab)])

    def tail_unit(u):
        it, h, ti, j0, j1, lastg = units[u]
        pt = h // 2
        b = (h % 2) * 64
        ab = AB[it % 2]
        tb = TB[it % 2]
        osb = ot[it % 2]
        dc = 40 + (it % 2)
        p.op('vector', lambda v, ab=ab, dc=dc: v.reciprocal(
                out=denr.ap()[:, dc:dc+1], in_=ab.ap()[:, 32:33]),
             reads=[tv(ab)], writes=[tv(denr, f=(dc, dc+1))])
        p.op('vector', lambda v, ab=ab, osb=osb, dc=dc: v.tensor_scalar(
                out=osb.ap()[:, 0:32], in0=ab.ap()[:, 0:32],
                scalar1=denr.ap()[:, dc:dc+1], scalar2=None, op0=ALU.mult),
             reads=[tv(ab), tv(denr, f=(dc, dc+1))], writes=[tv(osb, f=(0, 32))])
        p.op('tensor', lambda t, tb=tb, osb=osb: t.transpose(
                tb.ap()[0:32, 0:128], osb.ap()[:, 0:32], idf.ap()),
             reads=[tv(osb, f=(0, 32)), tv(idb)], writes=[tv(tb)])
        p.op('vector', lambda v, tb=tb, pt=pt, b=b, ti=ti: v.tensor_copy(
                out=out_cT.ap()[b:b+32, pt*T+ti*128:pt*T+(ti+1)*128], in_=tb.ap()[0:32, 0:128]),
             reads=[tv(tb)],
             writes=[tv(out_cT, p=(b, b+32), f=(pt*T+ti*128, pt*T+(ti+1)*128))])

    sc_unit(0)
    pend = None
    for u in range(U):
        if u + 1 < U:
            sc_unit(u + 1)
        rest_unit(u)
        if pend is not None:
            tail_unit(pend)
            pend = None
        if units[u][5]:
            pend = u
    if pend is not None:
        tail_unit(pend)

    owb = bufs['ow_l0']
    xre = bufs['xre']
    for ot in range(CT):
        if first:
            p.dma('gpsimd', lambda g, ot=ot: g.dma_start(
                    out=owb.ap().rearrange("q (ct o) -> q ct o", o=128),
                    in_=Wo.ap().rearrange("(ct q) o -> q ct o", q=128)[:, :, ot*128:(ot+1)*128]),
                  reads=[tv('dram_wo')], writes=[tv(owb)])
            p.dma('scalar', lambda s, ot=ot: s.dma_start(
                    out=woc.ap()[:, ot*1024:(ot+1)*1024], in_=owb.ap()),
                  reads=[tv(owb)], writes=[tv('dram_l0wo_bf')])
        else:
            p.dma('sync', lambda s, ot=ot: s.dma_start(
                    out=owb.ap(), in_=woc.ap()[:, ot*1024:(ot+1)*1024]),
                  reads=[tv('dram_l0wo_bf')], writes=[tv(owb)])
        for tc in range(NCH):
            def emit(t, ot=ot, tc=tc):
                last = None
                for ct in range(CT):
                    last = t.matmul(sps.ap(), owb.ap()[:, ct*128:(ct+1)*128],
                                    out_cT.ap()[:, ct*T+tc*512:ct*T+(tc+1)*512],
                                    start=(ct == 0), stop=(ct == CT-1))
                return last
            p.op('tensor', emit, reads=[tv(owb), tv(out_cT)], writes=[tv(sps)])
            xb = xre[(ot * NCH + tc) % 2]
            f0 = ot * T + tc * 512
            p.dma('sync', lambda s, f0=f0, xb=xb: s.dma_start(
                    out=xb.ap(), in_=dram['xspill'].ap()[:, f0:f0+512]),
                  reads=[tv('dram_xspill')], writes=[tv(xb)])
            p.op('vector', lambda v, f0=f0, xb=xb: v.tensor_add(
                    out=X.ap()[:, f0:f0+512], in0=sps.ap(), in1=xb.ap()),
                 reads=[tv(sps), tv(xb)], writes=[tv(X, f=(f0, f0+512))])



def mixes_from(p, bufs, src_tv_fn, CT, T, tc, outs, hprev, lnw, lnb, ones_key='ones_col_f32'):
    st = bufs['lnstats']; bca = bufs['lnbca']; xsq = bufs['lnxsq']; tmp2 = bufs['lntmp']
    onesc = bufs[ones_key]; onesf = bufs['ones_f32']
    sps = bufs['sps']
    C = CT * 128
    inv_c = 1.0 / C
    Srow = lambda k: tv(st, p=(0, 1), f=(k*512, (k+1)*512))
    sap = lambda k: st.ap()[0:1, k*512:(k+1)*512]
    def emit(t):
        last = None
        for ct in range(CT):
            ap, rtv = src_tv_fn(ct)
            last = t.matmul(sps.ap()[0:1, 0:512], onesc.ap(), ap, start=(ct == 0), stop=(ct == CT-1))
        return last
    p.op('tensor', emit, reads=[src_tv_fn(ct)[1] for ct in range(CT)] + [tv(onesc)], writes=[tv(sps)])
    p.op('vector', lambda v: v.tensor_copy(out=sap(0), in_=sps.ap()[0:1, 0:512]),
         reads=[tv(sps)], writes=[Srow(0)])
    for ct in range(CT):
        ap, rtv = src_tv_fn(ct)
        p.op('scalar', lambda s, ap=ap: s.activation(out=xsq.ap(), in_=ap, func=AF.Square),
             reads=[rtv], writes=[tv(xsq)])
        p.op('tensor', lambda t, ct=ct: t.matmul(
                sps.ap()[0:1, 0:512], bufs['ones_col_f32'].ap(), xsq.ap(),
                start=(ct == 0), stop=(ct == CT-1)),
             reads=[tv(xsq), tv(bufs['ones_col_f32'])], writes=[tv(sps)])
    p.op('vector', lambda v: v.tensor_copy(out=sap(1), in_=sps.ap()[0:1, 0:512]),
         reads=[tv(sps)], writes=[Srow(1)])
    p.op('vector', lambda v: v.tensor_scalar(out=sap(0), in0=sap(0), scalar1=inv_c,
                                             scalar2=None, op0=ALU.mult),
         reads=[Srow(0)], writes=[Srow(0)])
    p.op('vector', lambda v: v.tensor_mul(out=sap(2), in0=sap(0), in1=sap(0)),
         reads=[Srow(0)], writes=[Srow(2)])
    p.op('vector', lambda v: v.tensor_scalar(out=sap(1), in0=sap(1), scalar1=inv_c,
                                             scalar2=None, op0=ALU.mult),
         reads=[Srow(1)], writes=[Srow(1)])
    p.op('vector', lambda v: v.tensor_sub(out=sap(1), in0=sap(1), in1=sap(2)),
         reads=[Srow(1), Srow(2)], writes=[Srow(1)])
    p.op('scalar', lambda s: s.activation(out=sap(2), in_=sap(1), func=AF.Ln, bias=1e-5, scale=1.0),
         reads=[Srow(1)], writes=[Srow(2)])
    p.op('scalar', lambda s: s.activation(out=sap(3), in_=sap(2), func=AF.Exp, bias=0.0, scale=-0.5),
         reads=[Srow(2)], writes=[Srow(3)])
    for (k, dst0) in [(0, 0), (3, 512)]:
        p.op('tensor', lambda t, k=k: t.matmul(
                sps.ap()[:, 0:512], onesf.ap()[0:1, 0:128], sap(k), start=True, stop=True),
             reads=[Srow(k), tv(onesf)], writes=[tv(sps)])
        p.op('vector', lambda v, dst0=dst0: v.tensor_copy(
                out=bca.ap()[:, dst0:dst0+512], in_=sps.ap()[:, 0:512]),
             reads=[tv(sps)], writes=[tv(bca, f=(dst0, dst0+512))])
    for ct in range(CT):
        ap, rtv = src_tv_fn(ct)
        p.op('vector', lambda v, ap=ap: v.tensor_sub(out=tmp2.ap()[:, 0:512], in0=ap, in1=bca.ap()[:, 0:512]),
             reads=[rtv, tv(bca, f=(0, 512))], writes=[tv(tmp2, f=(0, 512))])
        p.op('vector', lambda v: v.tensor_mul(out=tmp2.ap()[:, 0:512], in0=tmp2.ap()[:, 0:512],
                                              in1=bca.ap()[:, 512:1024]),
             reads=[tv(tmp2, f=(0, 512)), tv(bca, f=(512, 1024))], writes=[tv(tmp2, f=(0, 512))])
        p.op('vector', lambda v, ct=ct: v.tensor_scalar(
                out=tmp2.ap()[:, 0:512], in0=tmp2.ap()[:, 0:512],
                scalar1=lnw.ap()[:, ct:ct+1], scalar2=lnb.ap()[:, ct:ct+1],
                op0=ALU.mult, op1=ALU.add),
             reads=[tv(tmp2, f=(0, 512)), tv(lnw, f=(ct, ct+1)), tv(lnb, f=(ct, ct+1))],
             writes=[tv(tmp2, f=(0, 512))])
        p.op('vector', lambda v: v.tensor_sub(out=tmp2.ap()[:, 513:1024], in0=tmp2.ap()[:, 1:512],
                                              in1=tmp2.ap()[:, 0:511]),
             reads=[tv(tmp2, f=(0, 512))], writes=[tv(tmp2, f=(513, 1024))])
        p.op('vector', lambda v, ct=ct: v.tensor_sub(out=tmp2.ap()[:, 512:513], in0=tmp2.ap()[:, 0:1],
                                                     in1=hprev.ap()[:, ct:ct+1]),
             reads=[tv(tmp2, f=(0, 1)), tv(hprev, f=(ct, ct+1))], writes=[tv(tmp2, f=(512, 513))])
        p.op('vector', lambda v, ct=ct: v.tensor_copy(out=hprev.ap()[:, ct:ct+1], in_=tmp2.ap()[:, 511:512]),
             reads=[tv(tmp2, f=(511, 512))], writes=[tv(hprev, f=(ct, ct+1))])
        for buf, mkey in outs:
            mc = bufs[mkey]
            p.op('vector', lambda v, ct=ct, buf=buf, mc=mc: v.scalar_tensor_tensor(
                    out=buf.ap()[:, ct*512:(ct+1)*512], in0=tmp2.ap()[:, 512:1024],
                    scalar=mc.ap()[:, ct:ct+1], in1=tmp2.ap()[:, 0:512],
                    op0=ALU.mult, op1=ALU.add),
                 reads=[tv(tmp2, f=(0, 1024)), tv(mc, f=(ct, ct+1))],
                 writes=[tv(buf, f=(ct*512, (ct+1)*512))])


def proj_chunk(p, bufs, dram, wname, srcbuf, dstbuf, tc, T, CT, act=None):
    sps = bufs['sps']
    w = dram[wname]
    for ot in range(CT):
        wt = bufs['wt2'][proj_chunk.cnt % 2]
        proj_chunk.cnt += 1
        p.dma('gpsimd', lambda g, ot=ot, wt=wt, w=w: g.dma_start(
                out=wt.ap().rearrange("q (ct o) -> q ct o", o=128),
                in_=w.ap().rearrange("(ct q) o -> q ct o", q=128)[:, :, ot*128:(ot+1)*128]),
              reads=[tv('dram_' + wname)], writes=[tv(wt)])
        def emit(t, ot=ot, wt=wt):
            last = None
            for ct in range(CT):
                last = t.matmul(sps.ap(), wt.ap()[:, ct*128:(ct+1)*128],
                                srcbuf.ap()[:, ct*512:(ct+1)*512],
                                start=(ct == 0), stop=(ct == CT-1))
            return last
        p.op('tensor', emit, reads=[tv(wt), tv(srcbuf)], writes=[tv(sps)])
        fn = act or AF.Copy
        p.op('scalar', lambda s, ot=ot, tc=tc, fn=fn: s.activation(
                out=dstbuf.ap()[:, ot*T+tc*512:ot*T+(tc+1)*512], in_=sps.ap(), func=fn),
             reads=[tv(sps)], writes=[tv(dstbuf, f=(ot*T+tc*512, ot*T+(tc+1)*512))])
proj_chunk.cnt = 0


def l1attn_phase(p, bufs, dram, CT, T):
    X = bufs['X']
    zb = bufs['zb']
    kT = bufs['kT_r']
    vT = bufs['vT_r']
    srT = bufs['srT_r']
    mixk = bufs['mixk_b']; mixv = bufs['mixv_b']; mixr = bufs['mixr_b']
    hprev = bufs['hprev']
    NCH = T // 512
    C = CT * 128

    p.op('vector', lambda v: v.memset(hprev.ap(), 0.0), reads=[], writes=[tv(hprev)])

    ewb = bufs['ewb']; ekb = bufs['ekb']; ekvb = bufs['ekvb']; Zb = bufs['Zb']; Z2b = bufs['Z2b']
    scar = bufs['scarry']
    dec = bufs['decay_c']; tfu = bufs['first_c']; ewc = bufs['ew_c']
    p.op('scalar', lambda s: s.activation(out=ewc.ap(), in_=dec.ap(), func=AF.Exp),
         reads=[tv(dec)], writes=[tv(ewc)])
    p.op('scalar', lambda s: s.activation(out=ewc.ap(), in_=ewc.ap(), func=AF.Exp, bias=0.0, scale=-1.0),
         reads=[tv(ewc)], writes=[tv(ewc)])

    def scan_chain(ct, qc):
        s0 = ct * 4
        f0 = ct * T + qc * 512
        kv_ = tv(kT, f=(f0, f0 + 512))
        vv_ = tv(vT, f=(f0, f0 + 512))
        p.op('vector', lambda v: v.memset(ewb.ap(), 1.0), reads=[], writes=[tv(ewb)])
        p.op('vector', lambda v, ct=ct: v.tensor_scalar(out=ewb.ap(), in0=ewb.ap(),
                                                        scalar1=ewc.ap()[:, ct:ct+1],
                                                        scalar2=None, op0=ALU.mult),
             reads=[tv(ewb), tv(ewc, f=(ct, ct+1))], writes=[tv(ewb)])
        p.op('scalar', lambda s, f0=f0: s.activation(out=ekb.ap(), in_=kT.ap()[:, f0:f0+512],
                                                     func=AF.Exp),
             reads=[kv_], writes=[tv(ekb)])
        p.op('vector', lambda v, f0=f0: v.tensor_mul(out=ekvb.ap(), in0=ekb.ap(),
                                                     in1=vT.ap()[:, f0:f0+512]),
             reads=[tv(ekb), vv_], writes=[tv(ekvb)])
        ini_z = 0.0 if qc == 0 else scar.ap()[:, s0:s0+1]
        ini_z2 = 0.0 if qc == 0 else scar.ap()[:, s0+1:s0+2]
        rd_ini = [] if qc == 0 else [tv(scar, f=(s0, s0+2))]
        p.op('vector', lambda v, ini_z=ini_z: v.tensor_tensor_scan(
                out=Zb.ap(), data0=ewb.ap(), data1=ekvb.ap(), initial=ini_z,
                op0=ALU.mult, op1=ALU.add),
             reads=[tv(ewb), tv(ekvb)] + rd_ini, writes=[tv(Zb)])
        p.op('vector', lambda v, ini_z2=ini_z2: v.tensor_tensor_scan(
                out=Z2b.ap(), data0=ewb.ap(), data1=ekb.ap(), initial=ini_z2,
                op0=ALU.mult, op1=ALU.add),
             reads=[tv(ewb), tv(ekb)] + rd_ini, writes=[tv(Z2b)])
        p.op('scalar', lambda s, f0=f0, ct=ct: s.activation(
                out=ekb.ap(), in_=kT.ap()[:, f0:f0+512], func=AF.Exp,
                bias=tfu.ap()[:, ct:ct+1], scale=1.0),
             reads=[kv_, tv(tfu, f=(ct, ct+1))], writes=[tv(ekb)])
        p.op('vector', lambda v, f0=f0: v.tensor_mul(out=ekvb.ap(), in0=ekb.ap(),
                                                     in1=vT.ap()[:, f0:f0+512]),
             reads=[tv(ekb), vv_], writes=[tv(ekvb)])
        p.op('vector', lambda v: v.tensor_add(out=ekvb.ap()[:, 1:512], in0=ekvb.ap()[:, 1:512],
                                              in1=Zb.ap()[:, 0:511]),
             reads=[tv(ekvb), tv(Zb)], writes=[tv(ekvb)])
        if qc > 0:
            p.op('vector', lambda v: v.tensor_add(out=ekvb.ap()[:, 0:1], in0=ekvb.ap()[:, 0:1],
                                                  in1=scar.ap()[:, s0+2:s0+3]),
                 reads=[tv(ekvb), tv(scar, f=(s0+2, s0+3))], writes=[tv(ekvb)])
        p.op('vector', lambda v: v.tensor_add(out=ekb.ap()[:, 1:512], in0=ekb.ap()[:, 1:512],
                                              in1=Z2b.ap()[:, 0:511]),
             reads=[tv(ekb), tv(Z2b)], writes=[tv(ekb)])
        if qc > 0:
            p.op('vector', lambda v: v.tensor_add(out=ekb.ap()[:, 0:1], in0=ekb.ap()[:, 0:1],
                                                  in1=scar.ap()[:, s0+3:s0+4]),
                 reads=[tv(ekb), tv(scar, f=(s0+3, s0+4))], writes=[tv(ekb)])
        if qc < NCH - 1:
            p.op('vector', lambda v: v.tensor_copy(out=scar.ap()[:, s0:s0+1], in_=Zb.ap()[:, 511:512]),
                 reads=[tv(Zb)], writes=[tv(scar, f=(s0, s0+1))])
            p.op('vector', lambda v: v.tensor_copy(out=scar.ap()[:, s0+1:s0+2], in_=Z2b.ap()[:, 511:512]),
                 reads=[tv(Z2b)], writes=[tv(scar, f=(s0+1, s0+2))])
            p.op('vector', lambda v: v.tensor_copy(out=scar.ap()[:, s0+2:s0+3], in_=scar.ap()[:, s0:s0+1]),
                 reads=[tv(scar, f=(s0, s0+1))], writes=[tv(scar, f=(s0+2, s0+3))])
            p.op('vector', lambda v: v.tensor_copy(out=scar.ap()[:, s0+3:s0+4], in_=scar.ap()[:, s0+1:s0+2]),
                 reads=[tv(scar, f=(s0+1, s0+2))], writes=[tv(scar, f=(s0+3, s0+4))])
        p.op('vector', lambda v: v.reciprocal(out=Zb.ap(), in_=ekb.ap()),
             reads=[tv(ekb)], writes=[tv(Zb)])
        p.op('vector', lambda v: v.tensor_mul(out=ekvb.ap(), in0=ekvb.ap(), in1=Zb.ap()),
             reads=[tv(ekvb), tv(Zb)], writes=[tv(ekvb)])
        p.op('vector', lambda v, f0=f0: v.tensor_mul(out=kT.ap()[:, f0:f0+512], in0=ekvb.ap(),
                                                     in1=srT.ap()[:, f0:f0+512]),
             reads=[tv(ekvb), tv(srT, f=(f0, f0+512))], writes=[kv_])

    for tc in range(NCH):
        def src(ct, tc=tc):
            ap = zb.ap()[:, ct*T+tc*512:ct*T+(tc+1)*512]
            return ap, tv(zb, f=(ct*T+tc*512, ct*T+(tc+1)*512))
        mixes_from(p, bufs, src, CT, T, tc,
                   [(mixk, 'amk_m1'), (mixv, 'amv_m1'), (mixr, 'amr_m1')],
                   hprev, bufs['ln1w'], bufs['ln1b'], ones_key='ones_col_bf16')
        proj_chunk(p, bufs, dram, 'att_WkT', mixk, kT, tc, T, CT)
        proj_chunk(p, bufs, dram, 'att_WvT', mixv, vT, tc, T, CT)
        proj_chunk(p, bufs, dram, 'att_WrT', mixr, srT, tc, T, CT, act=AF.Sigmoid)
        for ct in range(CT):
            scan_chain(ct, tc)
    p.full_barrier()

    y1a, y1b = bufs['y1a'], bufs['y1b']
    sps = bufs['sps']
    def y1buf(ct):
        return (y1a, ct) if ct < 4 else (y1b, ct - 4)
    for ot in range(CT):
        wo = bufs['wt2'][proj_chunk.cnt % 2]; proj_chunk.cnt += 1
        wsh = bufs['wt2'][proj_chunk.cnt % 2]; proj_chunk.cnt += 1
        p.dma('gpsimd', lambda g, ot=ot, wo=wo: g.dma_start(
                out=wo.ap().rearrange("q (ct o) -> q ct o", o=128),
                in_=dram['att_WoT'].ap().rearrange("(ct q) o -> q ct o", q=128)[:, :, ot*128:(ot+1)*128]),
              reads=[tv('dram_att_WoT')], writes=[tv(wo)])
        p.dma('gpsimd', lambda g, ot=ot, wsh=wsh: g.dma_start(
                out=wsh.ap().rearrange("q (ct o) -> q ct o", o=128),
                in_=dram['short_WT'].ap().rearrange("(ct q) o -> q ct o", q=128)[:, :, ot*128:(ot+1)*128]),
              reads=[tv('dram_short_WT')], writes=[tv(wsh)])
        for tc in range(NCH):
            def emit(t, ot=ot, tc=tc, wo=wo, wsh=wsh):
                last = None
                for ct in range(CT):
                    last = t.matmul(sps.ap(), wo.ap()[:, ct*128:(ct+1)*128],
                                    kT.ap()[:, ct*T+tc*512:ct*T+(tc+1)*512],
                                    start=(ct == 0), stop=False)
                for ct in range(CT):
                    last = t.matmul(sps.ap(), wsh.ap()[:, ct*128:(ct+1)*128],
                                    zb.ap()[:, ct*T+tc*512:ct*T+(tc+1)*512],
                                    start=False, stop=(ct == CT-1))
                return last
            p.op('tensor', emit, reads=[tv(wo), tv(wsh), tv(kT), tv(zb)], writes=[tv(sps)])
            yb, oti = y1buf(ot)
            p.op('vector', lambda v, yb=yb, oti=oti, tc=tc: v.tensor_copy(
                    out=yb.ap()[:, oti*T+tc*512:oti*T+(tc+1)*512], in_=sps.ap()),
                 reads=[tv(sps)], writes=[tv(yb, f=(oti*T+tc*512, oti*T+(tc+1)*512)),
                                          tv(vT), tv(srT)])
    p.full_barrier()

    gkb = bufs['mixk_b']; grb = bufs['mixr_b']
    h_buf = bufs['h_buf_r']
    W1b = bufs['W1b_r']; W2t = bufs['W2t_r']
    sr2 = bufs['mixv_b']
    relu_t = bufs['evtmp']
    xre = bufs['xre']
    ps = bufs['ps']
    HT = 32
    p.op('vector', lambda v: v.memset(hprev.ap(), 0.0), reads=[], writes=[tv(hprev)])
    for tc in range(NCH):
        def src(ct, tc=tc):
            yb, oti = y1buf(ct)
            ap = yb.ap()[:, oti*T+tc*512:oti*T+(tc+1)*512]
            return ap, tv(yb, f=(oti*T+tc*512, oti*T+(tc+1)*512))
        mixes_from(p, bufs, src, CT, T, tc,
                   [(gkb, 'fmk_m1'), (grb, 'fmr_m1')],
                   hprev, bufs['ln2w'], bufs['ln2b'])
        for ot in range(CT):
            wt = bufs['wt2'][proj_chunk.cnt % 2]; proj_chunk.cnt += 1
            p.dma('gpsimd', lambda g, ot=ot, wt=wt: g.dma_start(
                    out=wt.ap().rearrange("q (ct o) -> q ct o", o=128),
                    in_=dram['ffn_WrT'].ap().rearrange("(ct q) o -> q ct o", q=128)[:, :, ot*128:(ot+1)*128]),
                  reads=[tv('dram_ffn_WrT')], writes=[tv(wt)])
            def emit(t, ot=ot, wt=wt):
                last = None
                for ct in range(CT):
                    last = t.matmul(sps.ap(), wt.ap()[:, ct*128:(ct+1)*128],
                                    grb.ap()[:, ct*512:(ct+1)*512],
                                    start=(ct == 0), stop=(ct == CT-1))
                return last
            p.op('tensor', emit, reads=[tv(wt), tv(grb)], writes=[tv(sps)])
            p.op('scalar', lambda s, ot=ot: s.activation(out=sr2.ap()[:, ot*512:(ot+1)*512],
                                                         in_=sps.ap(), func=AF.Sigmoid),
                 reads=[tv(sps)], writes=[tv(sr2, f=(ot*512, (ot+1)*512))])
        for hg in range(16):
            wbuf = W1b[hg % 2]
            p.dma('gpsimd', lambda g, hg=hg, wbuf=wbuf: g.dma_start(
                    out=wbuf.ap().rearrange("q (ct h) -> q ct h", h=256),
                    in_=dram['ffn_WkT'].ap().rearrange("(ct q) h -> q ct h", q=128)[:, :, hg*256:(hg+1)*256]),
                  reads=[tv('dram_ffn_WkT')], writes=[tv(wbuf)])
            for i in range(2):
                ht = hg * 2 + i
                pp = ps[ht % 2]
                def emit(t, wbuf=wbuf, i=i, pp=pp):
                    last = None
                    for ct in range(CT):
                        last = t.matmul(pp.ap(), wbuf.ap()[:, ct*256+i*128:ct*256+(i+1)*128],
                                        gkb.ap()[:, ct*512:(ct+1)*512],
                                        start=(ct == 0), stop=(ct == CT-1))
                    return last
                p.op('tensor', emit, reads=[tv(wbuf), tv(gkb)], writes=[tv(pp)])
                p.op('scalar', lambda s, pp=pp: s.activation(
                        out=relu_t.ap(), in_=pp.ap(), func=AF.Relu),
                     reads=[tv(pp)], writes=[tv(relu_t)])
                p.op('vector', lambda v, ht=ht: v.tensor_mul(
                        out=h_buf.ap()[:, ht*512:(ht+1)*512], in0=relu_t.ap(), in1=relu_t.ap()),
                     reads=[tv(relu_t)], writes=[tv(h_buf, f=(ht*512, (ht+1)*512))])
        for ht in range(HT):
            w2b = W2t[ht % 2]
            p.dma('gpsimd', lambda g, ht=ht, w2b=w2b: g.dma_start(
                    out=w2b.ap(), in_=dram['ffn_WvT'].ap().rearrange("(i q) c -> q i c", q=128)[:, ht, :]),
                  reads=[tv('dram_ffn_WvT')], writes=[tv(w2b)])
            def emit(t, ht=ht, w2b=w2b):
                last = None
                for ct in range(CT):
                    last = t.matmul(ps[ct].ap(), w2b.ap()[:, ct*128:(ct+1)*128],
                                    h_buf.ap()[:, ht*512:(ht+1)*512],
                                    start=(ht == 0), stop=(ht == HT-1))
                return last
            p.op('tensor', emit,
                 reads=[tv(w2b), tv(h_buf, f=(ht*512, (ht+1)*512))],
                 writes=[tv(ps[c2]) for c2 in range(CT)])
        for ct in range(CT):
            pp = ps[ct]
            xb = xre[ct % 2]
            f0 = ct * T + tc * 512
            p.dma('sync', lambda s, f0=f0, xb=xb: s.dma_start(
                    out=xb.ap(), in_=dram['xspill'].ap()[:, f0:f0+512]),
                  reads=[tv('dram_xspill')], writes=[tv(xb)])
            yb, oti = y1buf(ct)
            p.op('vector', lambda v, ct=ct, pp=pp: v.tensor_mul(
                    out=relu_t.ap(), in0=pp.ap(), in1=sr2.ap()[:, ct*512:(ct+1)*512]),
                 reads=[tv(pp), tv(sr2, f=(ct*512, (ct+1)*512))], writes=[tv(relu_t)])
            p.op('vector', lambda v, yb=yb, oti=oti, tc=tc: v.tensor_add(
                    out=relu_t.ap(), in0=relu_t.ap(), in1=yb.ap()[:, oti*T+tc*512:oti*T+(tc+1)*512]),
                 reads=[tv(relu_t), tv(yb, f=(oti*T+tc*512, oti*T+(tc+1)*512))], writes=[tv(relu_t)])
            alias = [tv(bufs['kT_r'])] if ct < 4 else [tv(y1a, f=((ct-4)*T+tc*512, (ct-4)*T+(tc+1)*512))]
            p.op('vector', lambda v, f0=f0, xb=xb: v.tensor_add(
                    out=X.ap()[:, f0:f0+512], in0=relu_t.ap(), in1=xb.ap()),
                 reads=[tv(relu_t), tv(xb)], writes=[tv(X, f=(f0, f0+512))] + alias)

T_, C_, H_, NS_, NR_ = 2048, 1024, 4096, 2, 8
CT_ = C_ // 128
HT_ = H_ // 128
NT_ = T_ // 128
FCONST_VALS = [1e-5, float(np.finfo(np.float32).eps), -0.5, 0.5, 1e-6, 2.0, 1.0, 0.0,
               -1.0, 0.125, 3.0, 5.0, 7.0, 11.0, 13.0, 17.0]


def build_prog_fused():
    nc = bass.Bass('TRN2', target_bir_lowering=False, debug=False)
    dram = {}
    D = nc.dram_tensor
    dram['x_in'] = D('x_in', [T_, C_], F32, kind='ExternalInput')
    dram['W1_s'] = D('L0_ffn_W1_s', [NS_, C_, H_], F32, kind='ExternalInput')
    dram['b1_s'] = D('L0_ffn_b1_s', [NS_, H_], F32, kind='ExternalInput')
    dram['W2_s'] = D('L0_ffn_W2_s', [NS_, H_, C_], F32, kind='ExternalInput')
    dram['b2_s'] = D('L0_ffn_b2_s', [NS_, C_], F32, kind='ExternalInput')
    dram['W1_r'] = D('L0_ffn_W1_r', [NR_, C_, H_], F32, kind='ExternalInput')
    dram['b1_r'] = D('L0_ffn_b1_r', [NR_, H_], F32, kind='ExternalInput')
    dram['W2_r'] = D('L0_ffn_W2_r', [NR_, H_, C_], F32, kind='ExternalInput')
    dram['b2_r'] = D('L0_ffn_b2_r', [NR_, C_], F32, kind='ExternalInput')
    dram['g_shared'] = D('L0_ffn_g_shared', [C_], F32, kind='ExternalInput')
    dram['g_routed'] = D('L0_ffn_g_routed', [C_], F32, kind='ExternalInput')
    dram['centroids'] = D('L0_ffn_centroids', [C_, NR_], F32, kind='ExternalInput')
    dram['L0_Wq'] = D('L0_attn_Wq', [C_, C_], F32, kind='ExternalInput')
    dram['L0_Wkv'] = D('L0_attn_Wkv', [C_, 128], F32, kind='ExternalInput')
    dram['L0_Wk'] = D('L0_attn_Wk', [128, 512], F32, kind='ExternalInput')
    dram['L0_Wv'] = D('L0_attn_Wv', [128, 512], F32, kind='ExternalInput')
    dram['L0_Wo'] = D('L0_attn_Wo', [C_, C_], F32, kind='ExternalInput')
    dram['L0_q_g'] = D('L0_attn_q_g', [64], F32, kind='ExternalInput')
    dram['L0_k_g'] = D('L0_attn_k_g', [32], F32, kind='ExternalInput')
    dram['qkv_wT'] = D('L1f_qkv_wT', [C_, 3*C_], F32, kind='ExternalInput')
    dram['o_wT'] = D('L1f_o_wT', [C_, C_], F32, kind='ExternalInput')
    dram['n1_w'] = D('L1_ffn_n1_w', [C_], F32, kind='ExternalInput')
    dram['n2_w'] = D('L1_ffn_n2_w', [C_], F32, kind='ExternalInput')
    dram['gate_wT'] = D('L1f_gate_wT', [C_, NR_], F32, kind='ExternalInput')
    dram['up_wT'] = D('L1f_up_wT', [C_, H_], F32, kind='ExternalInput')
    dram['down_wT'] = D('L1f_down_wT', [H_, C_], F32, kind='ExternalInput')
    for nm, key, shp in [('L1a_att_WkT', 'att_WkT', [C_, C_]), ('L1a_att_WvT', 'att_WvT', [C_, C_]),
                         ('L1a_att_WrT', 'att_WrT', [C_, C_]), ('L1a_att_WoT', 'att_WoT', [C_, C_]),
                         ('L1a_short_WT', 'short_WT', [C_, C_]), ('L1a_ffn_WkT', 'ffn_WkT', [C_, H_]),
                         ('L1a_ffn_WvT', 'ffn_WvT', [H_, C_]), ('L1a_ffn_WrT', 'ffn_WrT', [C_, C_])]:
        dram[key] = D(nm, shp, F32, kind='ExternalInput')
    for nm, key in [('L1_attn_ln1_w', 'ln1_w'), ('L1_attn_ln1_b', 'ln1_b'),
                    ('L1_attn_ln2_w', 'ln2_w'), ('L1_attn_ln2_b', 'ln2_b'),
                    ('L1_attn_att_time_decay', 'att_time_decay'),
                    ('L1_attn_att_time_first', 'att_time_first'),
                    ('L1_attn_att_mix_k', 'att_mix_k'), ('L1_attn_att_mix_v', 'att_mix_v'),
                    ('L1_attn_att_mix_r', 'att_mix_r'), ('L1_attn_ffn_mix_k', 'ffn_mix_k'),
                    ('L1_attn_ffn_mix_r', 'ffn_mix_r')]:
        dram[key] = D(nm, [C_], F32, kind='ExternalInput')
    dram['ident_f32'] = D('c_ident', [128, 128], F32, kind='ExternalInput')
    dram['ident_bf16'] = D('c_identb', [128, 128], BF16, kind='ExternalInput')
    dram['ones_f32'] = D('c_onesr', [1, 128], F32, kind='ExternalInput')
    dram['ones_col_f32'] = D('c_onesc', [128, 1], F32, kind='ExternalInput')
    dram['ones_bf16'] = D('c_onesb', [1, 512], BF16, kind='ExternalInput')
    dram['ones_col_bf16'] = D('c_onescb', [128, 1], BF16, kind='ExternalInput')
    dram['fconsts'] = D('c_fconsts', [128, 16], F32, kind='ExternalInput')
    dram['esel'] = D('c_esel', [NR_, NR_*128], F32, kind='ExternalInput')
    dram['cmask'] = D('c_cmask', [128, 128], F32, kind='ExternalInput')
    dram['cm01T'] = D('c_cm01t', [128, 128], BF16, kind='ExternalInput')
    dram['xspill'] = D('xspill', [128, CT_*T_], F32, kind='Internal')
    dram['w1bf_s'] = D('w1bf_s', [NS_, 128, CT_*H_], BF16, kind='Internal')
    dram['w1bf_r'] = D('w1bf_r', [NR_, 128, CT_*H_], BF16, kind='Internal')
    dram['w2bf_s'] = D('w2bf_s', [NS_, 128, HT_*C_], BF16, kind='Internal')
    dram['w2bf_r'] = D('w2bf_r', [NR_, 128, HT_*C_], BF16, kind='Internal')
    dram['l0_wq_bf'] = D('l0_wq_bf', [128, C_*CT_], BF16, kind='Internal')
    dram['l0_wo_bf'] = D('l0_wo_bf', [128, C_*CT_], BF16, kind='Internal')
    dram['qkv_bf'] = D('qkv_bf', [128, 3*C_*CT_], BF16, kind='Internal')
    dram['o_bf'] = D('o_bf', [128, C_*CT_], BF16, kind='Internal')
    dram['up_bf'] = D('up_bf', [128, CT_*H_], BF16, kind='Internal')
    dram['down_bf'] = D('down_bf', [128, HT_*C_], BF16, kind='Internal')
    dram['y'] = D('y', [T_, C_], F32, kind='ExternalOutput')

    sb = Sbuf(nc)
    bufs = {}
    def A(key, shape, dtype, name=None):
        bufs[key] = sb.alloc(name or (key + '_sb'), shape, dtype)
    A('fconsts', [128, 16], F32)
    A('ident_f32', [128, 128], F32)
    A('ident_bf16', [128, 128], BF16)
    A('ones_f32', [1, 128], F32)
    A('ones_col_f32', [128, 1], F32)
    A('ones_bf16', [1, 512], BF16)
    A('ones_col_bf16', [128, 1], BF16)
    A('cmask', [128, 128], F32)
    A('cm01T', [128, 128], BF16)
    A('b1raw', [32, 128], F32)
    A('gate_work', [128, 48], F32)
    bufs['xre'] = [sb.alloc('xre0', [128, 512], F32), sb.alloc('xre1', [128, 512], F32)]
    A('g_shared_sb', [128, CT_], F32)
    A('g_routed_sb', [128, CT_], F32)
    A('gratio_sb', [128, CT_], F32)
    A('cent_sb', [128, CT_*NR_], F32)
    A('b1buf', [128, 32], F32)
    A('b2buf', [1, C_], BF16)
    A('n1w_sb', [128, CT_], F32)
    A('n2w_sb', [128, CT_], F32)
    for k in ['amk_m1', 'amv_m1', 'amr_m1', 'fmk_m1', 'fmr_m1', 'ln1w', 'ln1b',
              'ln2w', 'ln2b', 'decay_c', 'first_c', 'ew_c', 'hprev']:
        A(k, [128, CT_], F32)
    A('scarry', [128, 32], F32)
    A('evtmp', [128, 512], F32)
    A('evtmp2', [128, 512], F32)
    A('b2col', [128, 8], F32)
    A('esel', [NR_, NR_*128], F32, name='esel_sb')
    sb.mark('SLOT')
    A('X', [128, CT_*T_], F32)
    sb.mark('PH')

    A('u_hat', [128, CT_*T_], BF16)
    A('h_buf', [128, HT_*512], BF16)
    bufs['W1buf'] = [sb.alloc(f'W1buf{i}', [128, CT_*256], BF16) for i in range(4)]
    bufs['W2tile'] = [sb.alloc(f'W2tile{i}', [128, 2*C_], BF16) for i in range(3)]
    A('sc_tok', [128, NT_*8], F32)
    sb.mark('R')
    A('lnstats', [1, 4*512], F32)
    A('lnbca', [128, 3*512], F32)
    A('lntmp', [128, 2*512], F32)
    A('lnxsq', [128, 512], F32)
    print('fused hw (l0ffn stepA):', sb.off)
    sb.reset('R')
    A('gatesT', [NR_, T_], F32)
    A('g_b', [128, T_], F32)
    print('fused hw (l0ffn gates):', sb.off)
    sb.reset('R')
    bufs['xtok'] = [sb.alloc('xtok0', [128, C_], F32), sb.alloc('xtok1', [128, C_], F32)]
    print('fused hw (xtok):', sb.off)

    sb.reset('PH')
    A('xn', [128, CT_*T_], BF16, name='l0a_xn')
    A('kv_bf', [128, T_], BF16)
    A('v_tok', [128, NT_*512], BF16, name='l0a_vtok')
    sb.mark('L0R')
    A('l0a_lnstats', [1, 4*512], F32)
    A('l0a_lnbca', [128, 3*512], F32)
    A('l0a_lntmp', [128, 2*512], F32)
    A('l0a_lnxsq', [128, 512], F32)
    print('fused hw (l0a lnreg):', sb.off)
    sb.reset('L0R')
    bufs['eT_l0'] = [sb.alloc(f'l0a_eT{i}', [128, 512], BF16) for i in range(4)]
    bufs['ot_l0'] = [sb.alloc(f'l0a_ot{i}', [128, 64], F32) for i in range(2)]
    A('qtok', [128, C_], F32)
    A('qsq', [128, C_], F32)
    A('qinv', [128, 32], F32)
    A('qsc_b', [128, C_], F32)
    A('qg_row', [1, 64], F32)
    A('kg_row', [1, 64], F32)
    A('wkv_bf', [128, CT_*128], BF16)
    A('wk_bf', [128, 512], BF16)
    A('wv_bf', [128, 512], BF16)
    A('wq0', [128, CT_*512], BF16)
    A('ow_l0', [128, CT_*128], BF16)
    print('fused hw (l0a attn):', sb.off)
    sb.reset('SLOT')
    A('qsT', [128, CT_*T_], BF16)
    A('ksT', [128, CT_*T_], BF16)
    sb.reset('PH')
    A('out_cT', [128, CT_*T_], BF16, name='l0a_oc')

    sb.reset('PH')
    A('l1_xn', [128, CT_*T_], BF16)
    sb.mark('L1R')
    A('l1_lnstats', [1, 4*512], F32)
    A('l1_lnbca', [128, 3*512], F32)
    A('l1_lntmp', [128, 2*512], F32)
    A('l1_lnxsq', [128, 512], F32)
    print('fused hw (l1f lnreg):', sb.off)
    sb.reset('L1R')
    A('qT', [128, 2*T_], BF16)
    A('kT', [128, 2*T_], BF16)
    A('l1_vtok', [128, NT_*256], BF16)
    A('l1_oc', [128, CT_*T_], BF16)
    bufs['eT_l1'] = [sb.alloc(f'l1_eT{i}', [128, 512], BF16) for i in range(4)]
    bufs['ot_l1'] = [sb.alloc(f'l1_ot{i}', [128, 64], F32) for i in range(2)]
    A('wq1', [128, CT_*256], BF16)
    A('ow_b', [128, CT_*128], BF16)
    print('fused hw (l1f attn):', sb.off)
    sb.reset('L1R')
    A('l1_hbuf', [128, HT_*512], BF16)
    bufs['l1_W1buf'] = [sb.alloc(f'l1_W1buf{i}', [128, CT_*256], BF16) for i in range(3)]
    bufs['l1_W2tile'] = [sb.alloc(f'l1_W2tile{i}', [128, 2*C_], BF16) for i in range(3)]
    A('l1_gb', [128, T_], F32)
    A('fac', [128, 16], F32)
    A('facrow', [1, NT_*128], F32)
    A('gtw', [128, CT_*8], BF16)
    A('l1_sctok', [128, NT_*8], F32)
    print('fused hw (l1f moe):', sb.off)

    sb.reset('PH')
    bufs['wt2'] = [sb.alloc('wt2a', [128, CT_*128], BF16), sb.alloc('wt2b', [128, CT_*128], BF16)]
    sb.mark('ZB')
    A('zb', [128, CT_*T_], BF16)
    sb.reset('ZB')
    A('h_buf_r', [128, 32*512], BF16)
    sb.mark('SR')
    A('srT_r', [128, CT_*T_], BF16)
    sb.reset('SR')
    A('y1b', [128, 4*T_], F32)
    A('mixk_b', [128, CT_*512], BF16)
    A('mixv_b', [128, CT_*512], BF16)
    A('mixr_b', [128, CT_*512], BF16)
    A('l1a_lnstats', [1, 4*512], F32)
    A('l1a_lnbca', [128, 2*512], F32)
    A('l1a_lnxsq', [128, 512], F32)
    A('l1a_lntmp', [128, 2*512], F32)
    sb.mark('WREG')
    A('ewb', [128, 512], F32)
    A('ekb', [128, 512], F32)
    A('ekvb', [128, 512], F32)
    A('Zb', [128, 512], F32)
    A('Z2b', [128, 512], F32)
    print('fused hw (l1a scan):', sb.off)
    sb.reset('WREG')
    bufs['W1b_r'] = [sb.alloc(f'l1a_W1b{i}', [128, CT_*256], BF16) for i in range(2)]
    bufs['W2t_r'] = [sb.alloc(f'l1a_W2t{i}', [128, C_], BF16) for i in range(2)]
    print('fused hw (l1a ffnw):', sb.off)
    sb.reset('SLOT')
    A('kT_r', [128, CT_*T_], BF16)
    sb.mark('VSLOT')
    A('vT_r', [128, CT_*T_], BF16)
    sb.reset('VSLOT')
    A('y1a', [128, 4*T_], F32)

    P = nc.alloc_psum_tensor
    bufs['ps'] = [P(f'ps{i}', [128, 512], F32) for i in range(8)]
    bufs['hps0'] = bufs['ps'][0]; bufs['hps1'] = bufs['ps'][1]
    bufs['tps'] = bufs['ps'][2]; bufs['gps'] = bufs['ps'][3]; bufs['sps'] = bufs['ps'][4]

    common = bufs
    def view(**over):
        d = dict(common)
        d.update(over)
        return d
    bufs_l0f = view(lnstats=bufs['lnstats'], lnbca=bufs['lnbca'], lntmp=bufs['lntmp'],
                    lnxsq=bufs['lnxsq'])
    bufs_l0a = view(lnstats=bufs['l0a_lnstats'], lnbca=bufs['l0a_lnbca'],
                    lntmp=bufs['l0a_lntmp'], lnxsq=bufs['l0a_lnxsq'],
                    xn=bufs['xn'], v_tok=bufs['v_tok'], out_cT=bufs['out_cT'],
                    eT=bufs['eT_l0'], ot_sb=bufs['ot_l0'])
    bufs_l1f = view(lnstats=bufs['l1_lnstats'], lnbca=bufs['l1_lnbca'],
                    lntmp=bufs['l1_lntmp'], lnxsq=bufs['l1_lnxsq'],
                    xn=bufs['l1_xn'], v_tok=bufs['l1_vtok'], out_cT=bufs['l1_oc'],
                    eT=bufs['eT_l1'], ot_sb=bufs['ot_l1'],
                    h_buf=bufs['l1_hbuf'], W1buf=bufs['l1_W1buf'], W2tile=bufs['l1_W2tile'],
                    g_b=bufs['l1_gb'], sc_tok=bufs['l1_sctok'])
    bufs_l1a = view(lnstats=bufs['l1a_lnstats'], lnbca=bufs['l1a_lnbca'],
                    lntmp=bufs['l1a_lntmp'], lnxsq=bufs['l1a_lnxsq'])

    p = Prog(nc)
    stage_consts(p, nc, bufs, dram, FCONST_VALS)
    row_to_col(p, bufs, dram['g_shared'], None, CT_, tv(bufs['g_shared_sb']))
    row_to_col(p, bufs, dram['g_routed'], None, CT_, tv(bufs['g_routed_sb']))
    p.op('vector', lambda v: v.reciprocal(out=bufs['gratio_sb'].ap(), in_=bufs['g_shared_sb'].ap()),
         reads=[tv(bufs['g_shared_sb'])], writes=[tv(bufs['gratio_sb'])])
    p.op('vector', lambda v: v.tensor_mul(out=bufs['gratio_sb'].ap(), in0=bufs['gratio_sb'].ap(),
                                          in1=bufs['g_routed_sb'].ap()),
         reads=[tv(bufs['gratio_sb']), tv(bufs['g_routed_sb'])], writes=[tv(bufs['gratio_sb'])])
    p.dma('sync', lambda s: s.dma_start(
            out=bufs['cent_sb'].ap().rearrange("q (ct e) -> q ct e", e=NR_),
            in_=dram['centroids'].ap().rearrange("(ct q) e -> q ct e", q=128)),
          reads=[tv('dram_cent')], writes=[tv(bufs['cent_sb'])])
    row_to_col(p, bufs, dram['n1_w'], None, CT_, tv(bufs['n1w_sb']))
    row_to_col(p, bufs, dram['n2_w'], None, CT_, tv(bufs['n2w_sb']))
    for nm, key in [('ln1_w', 'ln1w'), ('ln1_b', 'ln1b'), ('ln2_w', 'ln2w'), ('ln2_b', 'ln2b'),
                    ('att_time_decay', 'decay_c'), ('att_time_first', 'first_c'),
                    ('att_mix_k', 'amk_m1'), ('att_mix_v', 'amv_m1'), ('att_mix_r', 'amr_m1'),
                    ('ffn_mix_k', 'fmk_m1'), ('ffn_mix_r', 'fmr_m1')]:
        row_to_col(p, bufs, dram[nm], None, CT_, tv(bufs[key]))
    for key in ['amk_m1', 'amv_m1', 'amr_m1', 'fmk_m1', 'fmr_m1']:
        p.op('vector', lambda v, key=key: v.tensor_scalar(
                out=bufs[key].ap(), in0=bufs[key].ap(), scalar1=-1.0, scalar2=None, op0=ALU.add),
             reads=[tv(bufs[key])], writes=[tv(bufs[key])])
    transpose_in(p, bufs, dram['x_in'], T_, CT_)
    p.full_barrier()

    def xn_copy(b_):
        for ct in range(CT_):
            p.op('scalar', lambda s, ct=ct, b_=b_: s.activation(
                    out=b_.ap()[:, ct*T_:(ct+1)*T_], in_=bufs['X'].ap()[:, ct*T_:(ct+1)*T_],
                    func=AF.Copy),
                 reads=[tv(bufs['X'], f=(ct*T_, (ct+1)*T_))], writes=[tv(b_, f=(ct*T_, (ct+1)*T_))])

    l0ffn_phase(p, bufs_l0f, dram, CT_, T_, H_, NS_, NR_, first=True)
    p.full_barrier()
    spill_X(p, bufs, dram, CT_, T_)
    ln_plain_inplace(p, bufs_l0a, CT_, T_)
    xn_copy(bufs['xn'])
    p.full_barrier()
    l0attn_phase(p, bufs_l0a, dram, CT_, T_, first=True)
    p.full_barrier()
    l0ffn_phase(p, bufs_l0f, dram, CT_, T_, H_, NS_, NR_, first=False)
    p.full_barrier()
    l1ffn_phase(p, bufs_l1f, dram, CT_, T_, H_, first=True)
    p.full_barrier()
    spill_X(p, bufs, dram, CT_, T_)
    ln_plain_inplace(p, bufs_l1a, CT_, T_)
    xn_copy(bufs['zb'])
    p.full_barrier()
    l1attn_phase(p, bufs_l1a, dram, CT_, T_)
    p.full_barrier()
    l1ffn_phase(p, bufs_l1f, dram, CT_, T_, H_, first=False)
    p.full_barrier()
    transpose_out(p, bufs, dram['y'], T_, CT_)
    p.wait_all('sync')
    p.build()
    print('fused program ops:', p.n_ops)
    return nc


_PROGS = {}


def _get_fused():
    if 'fused' not in _PROGS:
        _PROGS['fused'] = build_prog_fused()
    return _PROGS['fused']


def _host_consts():
    esel = np.zeros((NR_, NR_*128), np.float32)
    for e in range(NR_):
        esel[e, e*128:(e+1)*128] = 1.0
    cm = np.zeros((128, 128), np.float32)
    for i in range(128):
        cm[i, i+1:] = -1e30
    return {
        'c_fconsts': np.tile(np.array([FCONST_VALS], np.float32), (128, 1)),
        'c_esel': esel,
        'c_cmask': cm,
        'c_cm01t': np.triu(np.ones((128, 128))).astype(ml_dtypes.bfloat16),
        'c_ident': np.eye(128, dtype=np.float32),
        'c_identb': np.eye(128).astype(ml_dtypes.bfloat16),
        'c_onesr': np.ones((1, 128), np.float32),
        'c_onesc': np.ones((128, 1), np.float32),
        'c_onesb': np.ones((1, 512), ml_dtypes.bfloat16),
        'c_onescb': np.ones((128, 1), ml_dtypes.bfloat16),
    }


def kernel(**inputs):
    nc = _get_fused()
    consts = _host_consts()
    NC = 8
    base = dict(consts)
    _unused = {'L0_attn_x', 'L1_ffn_qkv_w', 'L1_ffn_o_w', 'L1_ffn_gate_w',
               'L1_ffn_up_w', 'L1_ffn_down_w', 'L1_attn_att_Wk', 'L1_attn_att_Wv',
               'L1_attn_att_Wr', 'L1_attn_att_Wo', 'L1_attn_short_W',
               'L1_attn_ffn_Wk', 'L1_attn_ffn_Wv', 'L1_attn_ffn_Wr'}
    for k, v in inputs.items():
        if k not in _unused:
            base[k] = np.ascontiguousarray(v)
    base['L1f_qkv_wT'] = np.ascontiguousarray(inputs['L1_ffn_qkv_w'].T)
    base['L1f_o_wT'] = np.ascontiguousarray(inputs['L1_ffn_o_w'].T)
    base['L1f_gate_wT'] = np.ascontiguousarray(inputs['L1_ffn_gate_w'].T)
    base['L1f_up_wT'] = np.ascontiguousarray(inputs['L1_ffn_up_w'].T)
    base['L1f_down_wT'] = np.ascontiguousarray(inputs['L1_ffn_down_w'].T)
    base['L1a_att_WkT'] = np.ascontiguousarray(inputs['L1_attn_att_Wk'].T)
    base['L1a_att_WvT'] = np.ascontiguousarray(inputs['L1_attn_att_Wv'].T)
    base['L1a_att_WrT'] = np.ascontiguousarray(inputs['L1_attn_att_Wr'].T)
    base['L1a_att_WoT'] = np.ascontiguousarray(inputs['L1_attn_att_Wo'].T)
    base['L1a_short_WT'] = np.ascontiguousarray(inputs['L1_attn_short_W'].T)
    base['L1a_ffn_WkT'] = np.ascontiguousarray(inputs['L1_attn_ffn_Wk'].T)
    base['L1a_ffn_WvT'] = np.ascontiguousarray(inputs['L1_attn_ffn_Wv'].T)
    base['L1a_ffn_WrT'] = np.ascontiguousarray(inputs['L1_attn_ffn_Wr'].T)

    x = np.ascontiguousarray(inputs['L0_attn_x'].astype(np.float32))
    in_maps = [dict(base, x_in=np.ascontiguousarray(x[b])) for b in range(NC)]
    res = run_bass_kernel_spmd(nc, in_maps, core_ids=list(range(NC)))
    return np.stack([np.asarray(res.results[b]['y']) for b in range(NC)], axis=0)
